# version 23
# baseline (speedup 1.0000x reference)
import numpy as np
import ml_dtypes
from contextlib import ExitStack

import concourse.bass as bass
import concourse.mybir as mybir
from concourse.bass_utils import run_bass_kernel_spmd

F32 = mybir.dt.float32
BF16 = mybir.dt.bfloat16
I16 = mybir.dt.int16
I32 = mybir.dt.int32
AF = mybir.ActivationFunctionType
ALU = mybir.AluOpType
AX = mybir.AxisListType

P = 128
D = 2048
H4 = 8192
NTOK = 1024
NT = NTOK // P
NB = 4096
NGT = NB // P
KT = D // P
OC = 512
NOC = H4 // OC
OT = OC // P
CT = D // P
EPS = 1e-5
MAGIC_MANT = 0x400000
THR_MANT = 0x7E0000
_PHASES = 4
_SUBA = 99
_DEBUG = False


class Sch:
    ENGS = ("T", "V", "S", "Q", "G", "H")

    def __init__(self, nc, stack):
        self.sem = {k: stack.enter_context(nc.semaphore("sem" + k)) for k in self.ENGS}
        self.cnt = {k: 0 for k in self.ENGS}
        self.waited = {(a, b): 0 for a in self.ENGS for b in self.ENGS}

    def mark(self, key, instr, n=1):
        instr.then_inc(self.sem[key], n)
        self.cnt[key] += n
        return self.cnt[key]

    def step(self, eng, key, instr, n=1):
        if key == "V":
            eng.tensor_copy(out=self.vdum[:, 2:3], in_=self.vdum[:, 0:1])
        elif key == "S":
            eng.copy(out=self.sdum[:, 2:3], in_=self.sdum[:, 0:1])
        else:
            c = self.mark(key, instr, n)
            eng.wait_ge(self.sem[key], c)
        return self.cnt[key]

    def wait(self, eng, skey, dkey, val):
        if not val or val <= 0:
            return
        if self.waited[(skey, dkey)] >= val:
            return
        eng.wait_ge(self.sem[dkey], val)
        self.waited[(skey, dkey)] = val


def build_nc():
    nc = bass.Bass("TRN2", target_bir_lowering=False, detect_race_conditions=False)

    xg = nc.dram_tensor("xg", [NB, D], F32, kind="ExternalInput").ap()
    xo = nc.dram_tensor("xo", [NTOK, D], F32, kind="ExternalInput").ap()
    wvt = nc.dram_tensor("wvt", [D, D], F32, kind="ExternalInput").ap()
    wot = nc.dram_tensor("wot", [D, D], F32, kind="ExternalInput").ap()
    w1 = nc.dram_tensor("w1", [H4, D], F32, kind="ExternalInput").ap()
    w2 = nc.dram_tensor("w2", [D, H4], F32, kind="ExternalInput").ap()
    bv_row = nc.dram_tensor("bv_row", [1, D], F32, kind="ExternalInput").ap()
    bo_row = nc.dram_tensor("bo_row", [1, D], F32, kind="ExternalInput").ap()
    b1_row = nc.dram_tensor("b1_row", [1, H4], F32, kind="ExternalInput").ap()
    b2_row = nc.dram_tensor("b2_row", [1, D], F32, kind="ExternalInput").ap()
    lnw_row = nc.dram_tensor("lnw_row", [1, D], F32, kind="ExternalInput").ap()
    lnb_row = nc.dram_tensor("lnb_row", [1, D], F32, kind="ExternalInput").ap()
    identb = nc.dram_tensor("identb", [P, P], BF16, kind="ExternalInput").ap()
    onesf = nc.dram_tensor("onesf", [1, 512], F32, kind="ExternalInput").ap()
    ones_col = nc.dram_tensor("ones_col", [P, 1], F32, kind="ExternalInput").ap()
    out = nc.dram_tensor("out", [NTOK, D], F32, kind="ExternalOutput").ap()

    if _DEBUG:
        dbg_attn = nc.dram_tensor("dbg_attn", [P, D], F32, kind="ExternalOutput").ap()
        dbg_lncol = nc.dram_tensor("dbg_lncol", [P, KT], F32, kind="ExternalOutput").ap()
        dbg_x1 = nc.dram_tensor("dbg_x1", [P, D], F32, kind="ExternalOutput").ap()
        dbg_xf = nc.dram_tensor("dbg_xf", [P, D], I16, kind="ExternalOutput").ap()
        dbg_qht = nc.dram_tensor("dbg_qht", [P, KT * P], I16, kind="ExternalOutput").ap()
        dbg_y2 = nc.dram_tensor("dbg_y2", [P, D], F32, kind="ExternalOutput").ap()
    x1d = nc.dram_tensor("x1d", [NTOK, D], F32, kind="Internal").ap()
    xfd = nc.dram_tensor("xfd", [NTOK, D], I16, kind="Internal").ap()
    qhtd = nc.dram_tensor("qhtd", [NTOK, KT * P], I16, kind="Internal").ap()

    stack = ExitStack()
    with stack:
        sch = Sch(nc, stack)
        T, V, S, Q, G, Hs = "T", "V", "S", "Q", "G", "H"

        def sb(name, shape, dt):
            return stack.enter_context(nc.sbuf_tensor(name, shape, dt))

        s_identb = sb("s_identb", [P, P], BF16)
        s_onesf = sb("s_onesf", [1, 512], F32)
        s_onescol = sb("s_onescol", [P, 1], F32)
        s_eps = sb("s_eps", [P, 1], F32)
        s_rowbuf = sb("s_rowbuf", [1, OC + D], F32)
        s_rowacc = sb("s_rowacc", [1, D], F32)

        s_stat = sb("s_stat", [P, 16], F32)
        s_corr = sb("s_corr", [1, 2], F32)
        s_lncol = sb("s_lncol", [P, KT], F32)
        s_vcol = sb("s_vcol", [P, KT], F32)
        s_lnrow = s_rowacc
        s_vrow = s_rowacc
        s_attnrow = s_rowacc

        s_x1a = sb("s_x1a", [P, D], F32)
        s_x1b = sb("s_x1b", [P, D], F32)
        s_xf = sb("s_xf", [P, D], I16)
        s_t2 = sb("s_t2", [P, D], F32)
        s_h = sb("s_h", [P, D], I16)
        s_qa = sb("s_qa", [P, D], F32)
        s_qb = sb("s_qb", [P, D], F32)
        s_qh = sb("s_qh", [P, D], I16)
        s_qhT = sb("s_qhT", [P, KT * P], I16)
        s_bstat = sb("s_bstat", [P, 16], F32)
        s_inj = sb("s_inj", [P, 8], I32)
        s_vdum = sb("s_vdum", [P, 4], F32)
        s_sdum = sb("s_sdum", [P, 4], F32)
        sch.vdum = s_vdum
        sch.sdum = s_sdum
        s_eb = sb("s_eb", [P, 64], I32)
        s_mb = sb("s_mb", [P, 64], I32)
        s_thr = sb("s_thr", [P, 64], I32)

        nb_h = D // 32
        nb_g = OC // 32

        s_w1a = sb("s_w1a", [P, D], F32)
        s_w1b = sb("s_w1b", [P, D], F32)
        s_qw = sb("s_qw", [P, D], I16)
        s_qw1t = sb("s_qw1t", [P, KT * OC], I16)
        s_qw2t = sb("s_qw2t", [P, OT * D], I16)
        s_y2 = sb("s_y2", [P, NT * D], F32)
        s_qht_l = sb("s_qht_l", [P, KT * P], I16)
        s_qht_l2 = sb("s_qht_l2", [P, KT * P], I16)
        s_h1 = s_qh[:, 0:OC]
        s_g = s_qh[:, OC:2 * OC]
        s_qg = s_qh[:, 2 * OC:3 * OC]
        s_qgT = s_qh[:, 3 * OC:4 * OC]
        s_b1rep = None

        s_b1rep = s_t2[:, 0:OC]
        s_lnwrep = s_y2[:, 2 * D:3 * D]
        s_lnbrep = s_y2[:, 3 * D:4 * D]
        s_attnrep = s_y2[:, 4 * D:5 * D]
        s_junk = s_qb
        s_s1 = s_qa
        s_w2a, s_w2b = s_x1a, s_x1b
        s_wv, s_wv2 = s_y2[:, 5 * D:6 * D], s_y2[:, 6 * D:7 * D]
        s_xga, s_xgb = s_y2[:, 0:D], s_y2[:, D:2 * D]

        with nc.Block() as block:

            def emit(engattr, fn):
                getattr(block, engattr)(fn)

            def barrier(keys=("T", "V", "S", "Q", "G")):
                snap = dict(sch.cnt)
                attr = {"T": "tensor", "V": "vector", "S": "scalar",
                        "Q": "sync", "G": "gpsimd"}
                for k in keys:
                    def _b(e, k=k, snap=snap):
                        for d in keys:
                            if d != k:
                                sch.wait(e, k, d, snap[d])
                    emit(attr[k], _b)

            def emit_quant(v, in_val_ap, n, nblk, out_i16_ap):
                sch.step(v, V, v.tensor_reduce(
                    out=s_qa[:, 0:nblk],
                    in_=in_val_ap.rearrange("p (b k) -> p b k", k=32),
                    axis=AX.X, op=ALU.max, apply_absolute_value=True))
                sch.step(v, V, v.tensor_scalar(
                    out=s_eb[:, 0:nblk], in0=s_qa[:, 0:nblk].bitcast(I32),
                    scalar1=23, scalar2=None, op0=ALU.logical_shift_right))
                sch.step(v, V, v.tensor_scalar(
                    out=s_eb[:, 0:nblk], in0=s_eb[:, 0:nblk],
                    scalar1=64, scalar2=17, op0=ALU.max, op1=ALU.add))
                sch.step(v, V, v.tensor_scalar(
                    out=s_mb[:, 0:nblk], in0=s_eb[:, 0:nblk],
                    scalar1=23, scalar2=MAGIC_MANT,
                    op0=ALU.logical_shift_left, op1=ALU.bitwise_or))
                sch.step(v, V, v.tensor_scalar(
                    out=s_thr[:, 0:nblk], in0=s_eb[:, 0:nblk],
                    scalar1=17, scalar2=None, op0=ALU.subtract))
                sch.step(v, V, v.tensor_scalar(
                    out=s_thr[:, 0:nblk], in0=s_thr[:, 0:nblk],
                    scalar1=23, scalar2=THR_MANT,
                    op0=ALU.logical_shift_left, op1=ALU.bitwise_or))
                thr_b = s_thr[:, 0:nblk].bitcast(F32).unsqueeze(-1).broadcast_to([P, nblk, 32])
                mb_b = s_mb[:, 0:nblk].bitcast(F32).unsqueeze(-1).broadcast_to([P, nblk, 32])
                in3 = in_val_ap.rearrange("p (b k) -> p b k", k=32)
                qa3 = s_qa[:, 0:n].rearrange("p (b k) -> p b k", k=32)
                qb3 = s_qb[:, 0:n].rearrange("p (b k) -> p b k", k=32)
                o3 = out_i16_ap.bitcast(BF16).rearrange("p (b k) -> p b k", k=32)
                sch.step(v, V, v.tensor_tensor(out=qa3, in0=in3, in1=thr_b, op=ALU.min))
                sch.step(v, V, v.tensor_tensor(out=qb3, in0=qa3, in1=mb_b, op=ALU.add))
                return v.tensor_tensor(out=o3, in0=qb3, in1=mb_b, op=ALU.subtract)

            te_bank_mark = {}

            def transpose_evac(ptrs, ntile, src_fn, dst_fn, src_ready_v=None,
                               dst_batch_fn=None):
                ngrp = (ntile + 3) // 4
                for grp in range(ngrp):
                    lo = grp * 4
                    hi = min(lo + 4, ntile)
                    ptr = ptrs[grp % len(ptrs)]

                    def _t(t, lo=lo, hi=hi, ptr=ptr, src_ready_v=src_ready_v):
                        if src_ready_v is not None:
                            sch.wait(t, T, V, src_ready_v)
                        sch.wait(t, T, S, te_bank_mark.get(id(ptr), 0))
                        mm = None
                        for i in range(lo, hi):
                            mm = t.transpose(ptr[:, (i - lo) * P:(i - lo) * P + P],
                                             src_fn(i), s_identb[:])
                        sch.mark(T, mm)
                    emit("tensor", _t)
                    te = sch.cnt[T]

                    def _s(sc, lo=lo, hi=hi, te=te, ptr=ptr):
                        sch.wait(sc, S, T, te)
                        if dst_batch_fn is not None:
                            mm = sc.copy(out=dst_batch_fn(lo, hi),
                                         in_=ptr[:, 0:(hi - lo) * P])
                        else:
                            mm = None
                            for i in range(lo, hi):
                                mm = sc.copy(out=dst_fn(i),
                                             in_=ptr[:, (i - lo) * P:(i - lo) * P + P])
                        sch.mark(S, mm)
                    emit("scalar", _s)
                    te_bank_mark[id(ptr)] = sch.cnt[S]

            def _q0(q):
                sch.mark(Q, q.dma_start(out=s_identb[:], in_=identb), 16)
                sch.mark(Q, q.dma_start(out=s_onesf[:], in_=onesf), 16)
                sch.mark(Q, q.dma_start(out=s_onescol[:], in_=ones_col), 16)
            emit("sync", _q0)

            def _g0(g):
                g.memset(s_vdum[:], 0.0)
                g.memset(s_sdum[:], 0.0)
                sch.mark(Hs, g.memset(s_eps[:], EPS))
            emit("gpsimd", _g0)

            p_tw = stack.enter_context(nc.psum_tensor("p_tw", [P, 1024], BF16))
            p_tg = stack.enter_context(nc.psum_tensor("p_tg", [P, 1024], BF16))
            w1bufs = [s_w1a, s_w1b, s_t2]
            w2bufs = [s_w2a, s_w2b]
            qwbufs = [s_qw, s_xf]
            w1_V_done = {}
            w2_V_done = {}
            qw_T_free = [0, 0]
            qw_par = [0]
            qw_ready = {}
            qw1_ready = {}
            qw2_ready = {}

            def emit_w_chunk(oc):
                emit_w1_chunk(oc)
                emit_w2_chunk(oc)

            def emit_w1_chunk(oc):
                for jo in range(OT):
                    gi = oc * OT + jo
                    buf = w1bufs[gi % 3]

                    def _q(q, gi=gi, buf=buf):
                        if gi >= 3:
                            sch.wait(q, Q, V, w1_V_done.get(gi - 3, 0))
                        sch.mark(Q, q.dma_start(out=buf[:],
                                                in_=w1[gi * P:(gi + 1) * P, :]), 16)
                    emit("sync", _q)
                    qd = sch.cnt[Q]

                    par = qw_par[0]
                    qw_par[0] ^= 1
                    qwb = qwbufs[par]

                    def _v(v, buf=buf, qd=qd, tf=qw_T_free[par], qwb=qwb):
                        sch.wait(v, V, Q, qd)
                        sch.wait(v, V, T, tf)
                        sch.mark(V, emit_quant(v, buf[:], D, nb_h, qwb[:]))
                    emit("vector", _v)
                    w1_V_done[gi] = sch.cnt[V]

                    transpose_evac(
                        [p_tw, p_tg], KT,
                        lambda i, qwb=qwb: qwb[:, i * P:(i + 1) * P].bitcast(BF16),
                        lambda i, jo=jo: s_qw1t[:, i * OC + jo * P:
                                                i * OC + jo * P + P].bitcast(BF16),
                        src_ready_v=sch.cnt[V],
                        dst_batch_fn=lambda lo, hi, jo=jo: s_qw1t[:].bitcast(BF16)
                            .rearrange("p (k c) -> p k c", c=OC)
                            [:, lo:hi, jo * P:jo * P + P])
                    qw_T_free[par] = sch.cnt[T]

                qw1_ready[oc] = sch.cnt[S]

            def emit_w2_chunk(oc):
                for jb in range(CT // 4):
                    buf = w2bufs[jb % 2]

                    def _q(q, jb=jb, buf=buf, oc=oc):
                        if jb >= 2:
                            sch.wait(q, Q, V, w2_V_done.get((oc, jb - 2), 0))
                        mm = None
                        for jl in range(4):
                            jc = jb * 4 + jl
                            mm = q.dma_start(
                                out=buf[:, jl * OC:(jl + 1) * OC],
                                in_=w2[jc * P:(jc + 1) * P, oc * OC:(oc + 1) * OC])
                            sch.mark(Q, mm, 16)
                    emit("sync", _q)
                    qd = sch.cnt[Q]

                    par = qw_par[0]
                    qw_par[0] ^= 1
                    qwb = qwbufs[par]

                    def _v(v, buf=buf, qd=qd, tf=qw_T_free[par], qwb=qwb):
                        sch.wait(v, V, Q, qd)
                        sch.wait(v, V, T, tf)
                        sch.mark(V, emit_quant(v, buf[:], D, 4 * nb_g, qwb[:]))
                    emit("vector", _v)
                    w2_V_done[(oc, jb)] = sch.cnt[V]

                    transpose_evac(
                        [p_tw, p_tg], 4 * OT,
                        lambda i, qwb=qwb: qwb[:, i * P:(i + 1) * P].bitcast(BF16),
                        lambda i, jb=jb: s_qw2t[:, (i % 4) * D + (jb * 4 + i // 4) * P:
                                                (i % 4) * D + (jb * 4 + i // 4) * P + P]
                        .bitcast(BF16),
                        src_ready_v=sch.cnt[V],
                        dst_batch_fn=lambda lo, hi, jb=jb: s_qw2t[:].bitcast(BF16)
                            .rearrange("p (k c) -> p k c", c=D)
                            [:, 0:hi - lo, (jb * 4 + lo // 4) * P:(jb * 4 + lo // 4) * P + P])
                    qw_T_free[par] = sch.cnt[T]

                qw2_ready[oc] = sch.cnt[S]

            with (
                nc.psum_tensor("p_a", [P, D], F32) as p_a,
                nc.psum_tensor("p_b", [P, 512], F32) as p_b,
            ):
                for (dst, src_hbm) in (
                    (s_lnwrep, lnw_row),
                    (s_lnbrep, lnb_row),
                ):
                    def _q(q, src_hbm=src_hbm):
                        sch.wait(q, Q, T, sch.cnt[T])
                        sch.mark(Q, q.dma_start(out=s_rowbuf[0:1, 0:D], in_=src_hbm), 16)
                    emit("sync", _q)
                    qd = sch.cnt[Q]

                    def _t(t, qd=qd):
                        sch.wait(t, T, Q, qd)
                        sch.wait(t, T, G, sch.cnt[G])
                        sch.wait(t, T, V, sch.cnt[V])
                        mm = None
                        for j in range(4):
                            mm = t.matmul(p_a[:, j * 512:(j + 1) * 512], s_onesf[0:1, 0:P],
                                          s_rowbuf[0:1, j * 512:(j + 1) * 512],
                                          start=True, stop=True, skip_group_check=True)
                        sch.mark(T, mm)
                    emit("tensor", _t)

                    def _v(v, dst=dst):
                        sch.wait(v, V, T, sch.cnt[T])
                        mm = None
                        for j in range(4):
                            mm = v.tensor_copy(out=dst[:, j * 512:(j + 1) * 512],
                                               in_=p_a[:, j * 512:(j + 1) * 512])
                        sch.mark(V, mm)
                    emit("vector", _v)

                xgbufs = [s_xga, s_xgb]
                tile_T_done = [0] * NGT

                for jt in range(NGT if _SUBA >= 1 else 0):
                    buf = xgbufs[jt % 2]

                    def _q(q, jt=jt, buf=buf):
                        if jt >= 2:
                            sch.wait(q, Q, T, tile_T_done[jt - 2])
                        sch.mark(Q, q.dma_start(out=buf[:], in_=xg[jt * P:(jt + 1) * P, :]), 16)
                    emit("sync", _q)
                    qd = sch.cnt[Q]

                    def _v(v, jt=jt, buf=buf, qd=qd):
                        sch.wait(v, V, "H", 1)
                        sch.wait(v, V, Q, qd)
                        if jt >= 1:
                            sch.wait(v, V, T, tile_T_done[jt - 1])
                        sch.mark(V, v.tensor_reduce(out=s_stat[:, 0:1], in_=buf[:],
                                                    axis=AX.X, op=ALU.add))
                    emit("vector", _v)
                    vd1 = sch.cnt[V]

                    def _s(s, jt=jt, buf=buf, qd=qd, vd1=vd1):
                        sch.wait(s, S, Q, qd)
                        sch.wait(s, S, V, vd1)
                        sch.mark(S, s.activation(out=s_junk[:], in_=buf[:], func=AF.Square,
                                                 accum_out=s_stat[:, 1:2]))
                    emit("scalar", _s)
                    sd1 = sch.cnt[S]

                    def _v2(v, sd1=sd1):
                        sch.wait(v, V, S, sd1)
                        v.tensor_scalar(out=s_stat[:, 2:3], in0=s_stat[:, 0:1],
                                        scalar1=1.0 / D, scalar2=None, op0=ALU.mult)
                        sch.step(v, V, v.tensor_scalar(out=s_stat[:, 3:4], in0=s_stat[:, 1:2],
                                                       scalar1=1.0 / D, scalar2=None, op0=ALU.mult))
                        sch.step(v, V, v.tensor_tensor(out=s_stat[:, 5:6], in0=s_stat[:, 2:3],
                                                       in1=s_stat[:, 2:3], op=ALU.mult))
                        sch.mark(V, v.tensor_tensor(out=s_stat[:, 4:5], in0=s_stat[:, 3:4],
                                                    in1=s_stat[:, 5:6], op=ALU.subtract))
                    emit("vector", _v2)
                    vd2 = sch.cnt[V]

                    def _s2(s, vd2=vd2):
                        sch.wait(s, S, "H", 1)
                        sch.wait(s, S, V, vd2)
                        sch.mark(S, s.activation(out=s_stat[:, 5:6], in_=s_stat[:, 4:5],
                                                 func=AF.Sqrt, bias=s_eps[:], scale=1.0))
                    emit("scalar", _s2)
                    sd2 = sch.cnt[S]

                    def _v3(v, sd2=sd2):
                        sch.wait(v, V, S, sd2)
                        sch.step(v, V, v.reciprocal(out=s_stat[:, 6:7], in_=s_stat[:, 5:6]))
                        sch.step(v, V, v.tensor_scalar(out=s_stat[:, 7:8], in0=s_stat[:, 6:7],
                                                       scalar1=1.0 / NB, scalar2=None, op0=ALU.mult))
                        sch.mark(V, v.tensor_tensor(out=s_stat[:, 8:9], in0=s_stat[:, 7:8],
                                                    in1=s_stat[:, 2:3], op=ALU.mult))
                    emit("vector", _v3)
                    vd3 = sch.cnt[V]

                    def _t(t, jt=jt, buf=buf, vd3=vd3):
                        sch.wait(t, T, V, vd3)
                        st = (jt == 0)
                        sp = (jt == NGT - 1)
                        for j in range(4):
                            t.matmul(p_a[0:1, j * 512:(j + 1) * 512], s_stat[:, 7:8],
                                     buf[:, j * 512:(j + 1) * 512], start=st, stop=sp,
                                     skip_group_check=True)
                        sch.mark(T, t.matmul(p_b[0:1, 300:301], s_stat[:, 8:9], s_onescol[:],
                                             start=st, stop=sp, skip_group_check=True))
                    emit("tensor", _t)
                    tile_T_done[jt] = sch.cnt[T]

                def _v4(v):
                    if _SUBA < 2:
                        return
                    sch.wait(v, V, T, sch.cnt[T])
                    for j in range(4):
                        v.tensor_copy(out=s_lnrow[0:1, j * 512:(j + 1) * 512],
                                      in_=p_a[0:1, j * 512:(j + 1) * 512])
                    sch.step(v, V, v.tensor_copy(out=s_corr[0:1, 0:1], in_=p_b[0:1, 300:301]))
                    sch.mark(V, v.tensor_scalar(out=s_corr[0:1, 1:2], in0=s_corr[0:1, 0:1],
                                                scalar1=-1.0, scalar2=None, op0=ALU.mult))
                emit("vector", _v4)

                def _t4(t):
                    if _SUBA < 3:
                        return
                    sch.wait(t, T, V, sch.cnt[V])
                    mm = None
                    for jc in range(KT):
                        t.matmul(p_b[:, jc:jc + 1], s_lnrow[0:1, jc * P:(jc + 1) * P],
                                 s_onesf[0:1, 0:1], start=True, stop=False,
                                 skip_group_check=True)
                        mm = t.matmul(p_b[:, jc:jc + 1], s_onesf[0:1, 0:P],
                                      s_corr[0:1, 1:2], start=False, stop=True,
                                      skip_group_check=True)
                    sch.mark(T, mm)
                emit("tensor", _t4)

                def _v5(v):
                    if _SUBA < 3:
                        return
                    sch.wait(v, V, T, sch.cnt[T])
                    sch.mark(V, v.tensor_copy(out=s_lncol[:], in_=p_b[:, 0:KT]))
                emit("vector", _v5)

                if _PHASES >= 3:
                    emit_w_chunk(0)

                wvbufs = [s_wv, s_wv2]
                wv_T_done = [0] * KT
                for jc in range(KT if _SUBA >= 4 else 0):
                    buf = wvbufs[jc % 2]

                    def _q(q, jc=jc, buf=buf):
                        if jc >= 2:
                            sch.wait(q, Q, T, wv_T_done[jc - 2])
                        sch.mark(Q, q.dma_start(out=buf[:], in_=wvt[jc * P:(jc + 1) * P, :]), 16)
                    emit("sync", _q)
                    qd = sch.cnt[Q]

                    def _t(t, jc=jc, buf=buf, qd=qd):
                        sch.wait(t, T, Q, qd)
                        sch.wait(t, T, V, sch.cnt[V])
                        st = (jc == 0)
                        sp = (jc == KT - 1)
                        mm = None
                        for j in range(4):
                            mm = t.matmul(p_a[0:1, j * 512:(j + 1) * 512], s_lncol[:, jc:jc + 1],
                                          buf[:, j * 512:(j + 1) * 512], start=st, stop=sp,
                                          skip_group_check=True)
                        sch.mark(T, mm)
                    emit("tensor", _t)
                    wv_T_done[jc] = sch.cnt[T]

                def _qbv(q):
                    if _SUBA < 4:
                        return
                    sch.wait(q, Q, T, sch.cnt[T])
                    sch.mark(Q, q.dma_start(out=s_rowbuf[0:1, 0:D], in_=bv_row), 16)
                emit("sync", _qbv)
                qbv = sch.cnt[Q]

                def _v6(v, qbv=qbv):
                    if _SUBA < 4:
                        return
                    sch.wait(v, V, T, sch.cnt[T])
                    sch.wait(v, V, Q, qbv)
                    mm = None
                    for j in range(4):
                        mm = v.tensor_copy(out=s_vrow[0:1, j * 512:(j + 1) * 512],
                                           in_=p_a[0:1, j * 512:(j + 1) * 512])
                    sch.mark(V, mm)
                emit("vector", _v6)

                def _t5(t):
                    if _SUBA < 4:
                        return
                    sch.wait(t, T, V, sch.cnt[V])
                    mm = None
                    for jc in range(KT):
                        t.matmul(p_b[:, 32 + jc:33 + jc], s_vrow[0:1, jc * P:(jc + 1) * P],
                                 s_onesf[0:1, 0:1], start=True, stop=False,
                                 skip_group_check=True)
                        mm = t.matmul(p_b[:, 32 + jc:33 + jc], s_rowbuf[0:1, jc * P:(jc + 1) * P],
                                      s_onesf[0:1, 0:1], start=False, stop=True,
                                      skip_group_check=True)
                    sch.mark(T, mm)
                emit("tensor", _t5)

                def _v7(v):
                    if _SUBA < 4:
                        return
                    sch.wait(v, V, T, sch.cnt[T])
                    sch.mark(V, v.tensor_copy(out=s_vcol[:], in_=p_b[:, 32:32 + KT]))
                emit("vector", _v7)

                wo_T_done = [0] * KT
                for jq in range(KT if _SUBA >= 5 else 0):
                    buf = wvbufs[jq % 2]

                    def _q(q, jq=jq, buf=buf):
                        prev = wo_T_done[jq - 2] if jq >= 2 else wv_T_done[-1]
                        sch.wait(q, Q, T, prev)
                        sch.mark(Q, q.dma_start(out=buf[:], in_=wot[jq * P:(jq + 1) * P, :]), 16)
                    emit("sync", _q)
                    qd = sch.cnt[Q]

                    def _t(t, jq=jq, buf=buf, qd=qd):
                        sch.wait(t, T, Q, qd)
                        sch.wait(t, T, V, sch.cnt[V])
                        st = (jq == 0)
                        sp = (jq == KT - 1)
                        mm = None
                        for j in range(4):
                            mm = t.matmul(p_a[0:1, j * 512:(j + 1) * 512], s_vcol[:, jq:jq + 1],
                                          buf[:, j * 512:(j + 1) * 512], start=st, stop=sp,
                                          skip_group_check=True)
                        sch.mark(T, mm)
                    emit("tensor", _t)
                    wo_T_done[jq] = sch.cnt[T]

                def _qbo(q):
                    if _SUBA < 5:
                        return
                    sch.wait(q, Q, V, sch.cnt[V])
                    sch.mark(Q, q.dma_start(out=s_rowbuf[0:1, 0:D], in_=bo_row), 16)
                emit("sync", _qbo)
                qbo = sch.cnt[Q]

                def _v8(v, qbo=qbo):
                    if _SUBA < 5:
                        return
                    sch.wait(v, V, T, sch.cnt[T])
                    sch.wait(v, V, Q, qbo)
                    mm = None
                    for j in range(4):
                        mm = v.tensor_copy(out=s_attnrow[0:1, j * 512:(j + 1) * 512],
                                           in_=p_a[0:1, j * 512:(j + 1) * 512])
                    sch.mark(V, mm)
                emit("vector", _v8)

                def _t6(t):
                    if _SUBA < 6:
                        return
                    sch.wait(t, T, V, sch.cnt[V])
                    mm = None
                    for j in range(4):
                        t.matmul(p_a[:, j * 512:(j + 1) * 512], s_onesf[0:1, 0:P],
                                 s_attnrow[0:1, j * 512:(j + 1) * 512],
                                 start=True, stop=False, skip_group_check=True)
                        mm = t.matmul(p_a[:, j * 512:(j + 1) * 512], s_onesf[0:1, 0:P],
                                      s_rowbuf[0:1, j * 512:(j + 1) * 512],
                                      start=False, stop=True, skip_group_check=True)
                    sch.mark(T, mm)
                emit("tensor", _t6)

                def _v9(v):
                    if _SUBA < 6:
                        return
                    sch.wait(v, V, T, sch.cnt[T])
                    mm = None
                    for j in range(4):
                        mm = v.tensor_copy(out=s_attnrep[:, j * 512:(j + 1) * 512],
                                           in_=p_a[:, j * 512:(j + 1) * 512])
                    sch.mark(V, mm)
                emit("vector", _v9)

            if _DEBUG:
                def _qdbg(q):
                    sch.wait(q, Q, V, sch.cnt[V])
                    sch.mark(Q, q.dma_start(out=dbg_attn, in_=s_attnrep[:]), 16)
                    sch.mark(Q, q.dma_start(out=dbg_lncol, in_=s_lncol[:]), 16)
                emit("sync", _qdbg)

            barrier()

            with (
                nc.psum_tensor("p_tr", [P, 1024], BF16) as p_tr,
                nc.psum_tensor("p_tr2", [P, 1024], BF16) as p_tr2,
            ):
                x1bufs = [s_x1a, s_x1b]
                xfbufs = [s_xf, s_qw]
                st1_S = {}
                st1_G = {}
                st2_S = {}
                st2_G = {}
                st2_T = {}
                st2_V = {}

                def stage1(jt):
                    xb = x1bufs[jt % 2]
                    xf = xfbufs[jt % 2]
                    par = jt % 2

                    def _q(q, jt=jt, xb=xb):
                        sch.wait(q, Q, G, st1_G.get(jt - 2, 0))
                        sch.mark(Q, q.dma_start(out=xb[:], in_=xo[jt * P:(jt + 1) * P, :]), 16)
                    emit("sync", _q)
                    qd = sch.cnt[Q]

                    def _v(v, xb=xb, qd=qd, jt=jt):
                        sch.wait(v, V, Q, qd)
                        sch.wait(v, V, S, st1_S.get(jt - 1, 0))
                        sch.step(v, V, v.tensor_tensor(out=xb[:], in0=xb[:],
                                                       in1=s_attnrep[:], op=ALU.add))
                        sch.mark(V, v.tensor_reduce(out=s_bstat[:, 0:1], in_=xb[:],
                                                    axis=AX.X, op=ALU.add))
                    emit("vector", _v)
                    vd = sch.cnt[V]

                    def _g(g, jt=jt, xb=xb, vd=vd):
                        sch.wait(g, G, V, vd)
                        sch.mark(G, g.dma_start(out=x1d[jt * P:(jt + 1) * P, :], in_=xb[:]), 16)
                    emit("gpsimd", _g)

                    def _s(s, xb=xb, vd=vd):
                        sch.wait(s, S, V, vd)
                        sch.mark(S, s.activation(out=s_w1a[:], in_=xb[:], func=AF.Square,
                                                 accum_out=s_bstat[:, 1:2]))
                    emit("scalar", _s)
                    sd = sch.cnt[S]

                    def _v2(v, sd=sd):
                        sch.wait(v, V, S, sd)
                        v.tensor_scalar(out=s_bstat[:, 2:3], in0=s_bstat[:, 0:1],
                                        scalar1=1.0 / D, scalar2=None, op0=ALU.mult)
                        sch.step(v, V, v.tensor_scalar(out=s_bstat[:, 3:4], in0=s_bstat[:, 1:2],
                                                       scalar1=1.0 / D, scalar2=None, op0=ALU.mult))
                        sch.step(v, V, v.tensor_tensor(out=s_bstat[:, 12:13], in0=s_bstat[:, 2:3],
                                                       in1=s_bstat[:, 2:3], op=ALU.mult))
                        sch.mark(V, v.tensor_tensor(out=s_bstat[:, 4:5], in0=s_bstat[:, 3:4],
                                                    in1=s_bstat[:, 12:13], op=ALU.subtract))
                    emit("vector", _v2)
                    vd2 = sch.cnt[V]

                    def _s2(s, vd2=vd2):
                        sch.wait(s, S, V, vd2)
                        sch.mark(S, s.activation(out=s_bstat[:, 5:6], in_=s_bstat[:, 4:5],
                                                 func=AF.Sqrt, bias=s_eps[:], scale=1.0))
                    emit("scalar", _s2)
                    sd2 = sch.cnt[S]

                    def _v3(v, sd2=sd2):
                        sch.wait(v, V, S, sd2)
                        sch.step(v, V, v.reciprocal(out=s_bstat[:, 6:7], in_=s_bstat[:, 5:6]))
                        sch.step(v, V, v.tensor_tensor(out=s_bstat[:, 13:14], in0=s_bstat[:, 2:3],
                                                       in1=s_bstat[:, 6:7], op=ALU.mult))
                        sch.mark(V, v.tensor_scalar(out=s_bstat[:, 7:8], in0=s_bstat[:, 13:14],
                                                    scalar1=-1.0, scalar2=None, op0=ALU.mult))
                    emit("vector", _v3)
                    vd3 = sch.cnt[V]

                    def _s3(s, xb=xb, vd3=vd3, xf=xf, par=par, jt=jt):
                        sch.wait(s, S, V, max(vd3, st2_V.get(jt - 2, 0)))
                        sch.wait(s, S, G, st1_G.get(jt - 2, 0))
                        sch.mark(S, s.activation(out=xf[:].bitcast(BF16), in_=xb[:],
                                                 func=AF.Identity, bias=s_bstat[:, 7:8],
                                                 scale=s_bstat[:, 6:7],
                                                 accum_out=s_bstat[:, 8 + par:9 + par]))
                    emit("scalar", _s3)
                    st1_S[jt] = sch.cnt[S]

                    def _g2(g, jt=jt, xf=xf):
                        sch.wait(g, G, S, st1_S[jt])
                        sch.mark(G, g.dma_start(out=xfd[jt * P:(jt + 1) * P, :], in_=xf[:]), 16)
                    emit("gpsimd", _g2)
                    st1_G[jt] = sch.cnt[G]

                def stage2(jt):
                    xf = xfbufs[jt % 2]
                    par = jt % 2

                    def _v4(v, jt=jt, par=par):
                        sch.wait(v, V, S, st1_S[jt])
                        sch.mark(V, v.tensor_scalar(out=s_inj[:, 1:2].bitcast(F32),
                                                    in0=s_bstat[:, 8 + par:9 + par],
                                                    scalar1=-1.0 / D, scalar2=None,
                                                    op0=ALU.mult))
                    emit("vector", _v4)
                    vd4 = sch.cnt[V]

                    def _s4(s, vd4=vd4, xf=xf):
                        sch.wait(s, S, V, vd4)
                        sch.step(s, S, s.activation(out=s_s1[:], in_=xf[:].bitcast(BF16),
                                                    func=AF.Identity,
                                                    bias=s_inj[:, 1:2].bitcast(F32),
                                                    scale=1.0))
                        sch.mark(S, s.activation(out=s_w1b[:], in_=s_s1[:], func=AF.Square,
                                                 accum_out=s_bstat[:, 10:11]))
                    emit("scalar", _s4)
                    sd4 = sch.cnt[S]

                    def _v5(v, sd4=sd4):
                        sch.wait(v, V, S, sd4)
                        sch.mark(V, v.tensor_scalar(out=s_bstat[:, 11:12],
                                                    in0=s_bstat[:, 10:11],
                                                    scalar1=1.0 / D, scalar2=None,
                                                    op0=ALU.mult))
                    emit("vector", _v5)
                    vd5 = sch.cnt[V]

                    def _s5(s, vd5=vd5):
                        sch.wait(s, S, V, vd5)
                        sch.mark(S, s.activation(out=s_bstat[:, 14:15], in_=s_bstat[:, 11:12],
                                                 func=AF.Sqrt, bias=s_eps[:], scale=1.0))
                    emit("scalar", _s5)
                    sd5 = sch.cnt[S]

                    def _v6(v, sd5=sd5):
                        sch.wait(v, V, S, sd5)
                        sch.mark(V, v.reciprocal(out=s_inj[:, 0:1].bitcast(F32),
                                                 in_=s_bstat[:, 14:15]))
                    emit("vector", _v6)
                    vd6 = sch.cnt[V]

                    def _s6(s, vd6=vd6):
                        sch.wait(s, S, V, vd6)
                        sch.mark(S, s.activation(out=s_t2[:], in_=s_s1[:], func=AF.Identity,
                                                 scale=s_inj[:, 0:1].bitcast(F32)))
                    emit("scalar", _s6)
                    sd6 = sch.cnt[S]

                    def _v7(v, sd6=sd6, jt=jt):
                        sch.wait(v, V, S, sd6)
                        sch.wait(v, V, T, st2_T.get(jt - 1, 0))
                        sch.step(v, V, v.tensor_tensor(out=s_t2[:], in0=s_t2[:],
                                                       in1=s_lnwrep[:], op=ALU.mult))
                        sch.step(v, V, v.tensor_tensor(out=s_h[:].bitcast(BF16), in0=s_t2[:],
                                                       in1=s_lnbrep[:], op=ALU.add))
                        sch.mark(V, emit_quant(v, s_h[:].bitcast(BF16), D, nb_h, s_qh[:]))
                    emit("vector", _v7)

                    def _sw(sc, jt=jt):
                        sch.wait(sc, S, G, st2_G.get(jt - 1, 0))
                    emit("scalar", _sw)

                    transpose_evac(
                        [p_tr, p_tr2], KT,
                        lambda i: s_qh[:, i * P:(i + 1) * P].bitcast(BF16),
                        lambda i: s_qhT[:, i * P:(i + 1) * P].bitcast(BF16),
                        src_ready_v=sch.cnt[V],
                        dst_batch_fn=lambda lo, hi: s_qhT[:, lo * P:hi * P].bitcast(BF16))
                    st2_T[jt] = sch.cnt[T]
                    st2_S[jt] = sch.cnt[S]
                    st2_V[jt] = sch.cnt[V]
                    sdq = sch.cnt[S]

                    def _g3(g, jt=jt, sdq=sdq):
                        sch.wait(g, G, S, sdq)
                        sch.mark(G, g.dma_start(out=qhtd[jt * P:(jt + 1) * P, :],
                                                in_=s_qhT[:]), 16)
                    emit("gpsimd", _g3)
                    st2_G[jt] = sch.cnt[G]

                for step_i in range(NT + 1 if _PHASES >= 2 else 0):
                    if step_i < NT:
                        stage1(step_i)
                    if step_i >= 1:
                        stage2(step_i - 1)

            if _DEBUG:
                def _qdbg2(q):
                    sch.wait(q, Q, G, sch.cnt[G])
                    sch.mark(Q, q.dma_start(out=dbg_x1, in_=x1d[0:P, :]), 16)
                    sch.mark(Q, q.dma_start(out=dbg_xf, in_=xfd[0:P, :]), 16)
                    sch.mark(Q, q.dma_start(out=dbg_qht, in_=qhtd[0:P, :]), 16)
                emit("sync", _qdbg2)

            barrier()

            with (
                nc.psum_tensor("p_y1", [P, OC], F32) as p_y1,
                nc.psum_tensor("p_y1b", [P, OC], F32) as p_y1b,
                nc.psum_tensor("p_y2", [P, D], F32) as p_y2,
            ):
                qhlbufs = [s_qht_l, s_qht_l2]
                p_y1s = [p_y1, p_y1b]
                ffn1_T_done = {}
                h1_mark = {}
                y2_mark = {}

                for oc in range(NOC if _PHASES >= 3 else 0):
                    if oc > 0:
                        emit_w1_chunk(oc)
                    qw_ready_S = qw1_ready[oc]
                    b1rep_v = 0

                    def _qb1(q, oc=oc):
                        sch.wait(q, Q, T, sch.cnt[T])
                        sch.mark(Q, q.dma_start(out=s_rowbuf[0:1, 0:OC],
                                                in_=b1_row[0:1, oc * OC:(oc + 1) * OC]), 16)
                        if oc == 0:
                            sch.mark(Q, q.dma_start(out=s_rowbuf[0:1, OC:OC + D],
                                                    in_=b2_row), 16)
                    emit("sync", _qb1)
                    qb1 = sch.cnt[Q]

                    def emit_ffn1(jt, oc=oc):
                        qhl = qhlbufs[jt % 2]

                        def _g(g, oc=oc, jt=jt, qhl=qhl):
                            if jt >= 2:
                                sch.wait(g, G, T, ffn1_T_done.get((oc, jt - 2), 0))
                            elif oc >= 1:
                                sch.wait(g, G, T, ffn1_T_done.get((oc - 1, NT - 2 + jt), 0))
                            sch.mark(G, g.dma_start(out=qhl[:],
                                                    in_=qhtd[jt * P:(jt + 1) * P, :]), 16)
                        emit("gpsimd", _g)
                        gd = sch.cnt[G]
                        py1 = p_y1s[jt % 2]

                        def _t(t, qhl=qhl, gd=gd, oc=oc, jt=jt, py1=py1, qb1=qb1):
                            sch.wait(t, T, G, gd)
                            sch.wait(t, T, Q, qb1)
                            sch.wait(t, T, S, max(qw_ready_S,
                                     h1_mark.get((oc, jt - 2),
                                     h1_mark.get((oc - 1, NT - 2 + jt), 0))))
                            t.matmul(py1[:, 0:512], s_onesf[0:1, 0:P],
                                     s_rowbuf[0:1, 0:512],
                                     start=True, stop=False, skip_group_check=True)
                            mm = None
                            for jk in range(KT):
                                sp = (jk == KT - 1)
                                lhs = qhl[:, jk * P:(jk + 1) * P].bitcast(BF16)
                                mm = t.matmul(py1[:, 0:512], lhs,
                                              s_qw1t[:, jk * OC:
                                                     jk * OC + 512].bitcast(BF16),
                                              start=False, stop=sp, skip_group_check=True)
                            sch.mark(T, mm)
                        emit("tensor", _t)
                        ffn1_T_done[(oc, jt)] = sch.cnt[T]

                    def emit_epi(jt, oc=oc):
                        td = ffn1_T_done[(oc, jt)]
                        py1 = p_y1s[jt % 2]

                        def _s(s, td=td, py1=py1):
                            sch.wait(s, S, T, td)
                            sch.step(s, S, s.copy(out=s_h1[:].bitcast(BF16),
                                                  in_=py1[:, 0:OC]))
                            sch.mark(S, s.activation(out=s_g[:].bitcast(BF16),
                                                     in_=s_h1[:].bitcast(BF16),
                                                     func=AF.Gelu_apprx_tanh))
                        emit("scalar", _s)
                        h1_mark[(oc, jt)] = sch.cnt[S]
                        sd = sch.cnt[S]

                        def _v2(v, sd=sd):
                            sch.wait(v, V, S, sd)
                            sch.mark(V, emit_quant(v, s_g[:].bitcast(BF16), OC, nb_g, s_qg[:]))
                        emit("vector", _v2)

                        transpose_evac(
                            [p_tw, p_tg], OT,
                            lambda i: s_qg[:, i * P:(i + 1) * P].bitcast(BF16),
                            lambda i: s_qgT[:, i * P:(i + 1) * P].bitcast(BF16),
                            src_ready_v=sch.cnt[V],
                            dst_batch_fn=lambda lo, hi: s_qgT[:, lo * P:hi * P].bitcast(BF16))
                        sd3 = sch.cnt[S]

                        t3_marks = []

                        def _t3(t, sd3=sd3, oc=oc, jt=jt):
                            sch.wait(t, T, S, max(sd3, qw2_ready[oc]))
                            prev = y2_mark.get((oc, jt - 1),
                                   y2_mark.get((oc - 1, NT - 1), 0))
                            sch.wait(t, T, V, prev)
                            for jn in range(4):
                                if oc == 0:
                                    t.matmul(p_y2[:, jn * 512:(jn + 1) * 512],
                                             s_onesf[0:1, 0:P],
                                             s_rowbuf[0:1, OC + jn * 512:
                                                      OC + (jn + 1) * 512],
                                             start=True, stop=False,
                                             skip_group_check=True)
                                mm = None
                                for jo in range(OT):
                                    st = (jo == 0) and (oc != 0)
                                    sp = (jo == OT - 1)
                                    lhs = s_qgT[:, jo * P:(jo + 1) * P].bitcast(BF16)
                                    mm = t.matmul(p_y2[:, jn * 512:(jn + 1) * 512], lhs,
                                                  s_qw2t[:, jo * D + jn * 512:
                                                         jo * D + (jn + 1) * 512].bitcast(BF16),
                                                  start=st, stop=sp, skip_group_check=True)
                                t3_marks.append(sch.mark(T, mm))
                        emit("tensor", _t3)

                        def _v4(v, oc=oc, jt=jt, t3_marks=t3_marks):
                            dst = s_y2[:, jt * D:(jt + 1) * D]
                            mm = None
                            for j in range(4):
                                sch.wait(v, V, T, t3_marks[j])
                                sl = slice(j * 512, (j + 1) * 512)
                                if oc == 0:
                                    mm = v.tensor_copy(out=dst[:, sl], in_=p_y2[:, sl])
                                else:
                                    mm = v.tensor_tensor(out=dst[:, sl], in0=dst[:, sl],
                                                         in1=p_y2[:, sl], op=ALU.add)
                            sch.mark(V, mm)
                        emit("vector", _v4)
                        y2_mark[(oc, jt)] = sch.cnt[V]

                    for step in range(NT + 1):
                        if step < NT:
                            emit_ffn1(step)
                        if step == 0 and oc > 0:
                            emit_w2_chunk(oc)
                        if step >= 1:
                            emit_epi(step - 1)

            if _DEBUG:
                def _qdbg3(q):
                    sch.wait(q, Q, V, sch.cnt[V])
                    sch.mark(Q, q.dma_start(out=dbg_y2, in_=s_y2[:, 0:D]), 16)
                emit("sync", _qdbg3)

            barrier()

            d_V_done = [0] * NT
            for jt in range(NT if _PHASES >= 4 else 0):
                xb = [s_x1a, s_x1b][jt % 2]

                def _g(g, jt=jt, xb=xb):
                    if jt >= 1:
                        sch.wait(g, G, V, d_V_done[jt - 1])
                    sch.mark(G, g.dma_start(out=xb[:], in_=x1d[jt * P:(jt + 1) * P, :]), 16)
                    sch.mark(G, g.dma_start(out=s_xf[:],
                                            in_=xfd[jt * P:(jt + 1) * P, :]), 16)
                emit("gpsimd", _g)
                gd = sch.cnt[G]

                def _sD(s, jt=jt):
                    if jt >= 1:
                        sch.wait(s, S, V, d_V_done[jt - 1])
                    sch.mark(S, s.activation(out=s_h[:].bitcast(BF16),
                                             in_=s_y2[:, jt * D:(jt + 1) * D],
                                             func=AF.Identity))
                emit("scalar", _sD)
                sdD = sch.cnt[S]

                def _v(v, jt=jt, xb=xb, gd=gd, sdD=sdD):
                    sch.wait(v, V, G, gd)
                    sch.wait(v, V, S, sdD)
                    sch.step(v, V, v.tensor_tensor(out=s_qh[:].bitcast(BF16),
                                                   in0=s_xf[:].bitcast(BF16),
                                                   in1=s_h[:].bitcast(BF16), op=ALU.add))
                    sch.mark(V, v.tensor_tensor(out=s_t2[:], in0=xb[:],
                                                in1=s_qh[:].bitcast(BF16), op=ALU.add))
                emit("vector", _v)
                d_V_done[jt] = sch.cnt[V]
                vd = sch.cnt[V]

                def _q(q, jt=jt, vd=vd):
                    sch.wait(q, Q, V, vd)
                    sch.mark(Q, q.dma_start(out=out[jt * P:(jt + 1) * P, :], in_=s_t2[:]), 16)
                emit("sync", _q)
                qd2 = sch.cnt[Q]

                def _vw(v, qd2=qd2):
                    sch.wait(v, V, Q, qd2)
                emit("vector", _vw)

            if _PHASES < 4:
                for jt in range(NT):
                    def _q(q, jt=jt):
                        sch.wait(q, Q, V, sch.cnt[V])
                        sch.mark(Q, q.dma_start(out=out[jt * P:(jt + 1) * P, :],
                                                in_=s_t2[:]), 16)
                    emit("sync", _q)

            final_q = sch.cnt[Q]

            def _qf(q):
                q.wait_ge(sch.sem[Q], final_q)
            emit("sync", _qf)

    return nc


def kernel(L0_attn_x, L0_attn_Wq, L0_attn_bq, L0_attn_Wk, L0_attn_bk,
           L0_attn_Wv, L0_attn_bv, L0_attn_Wo, L0_attn_bo,
           L0_ffn_ln_w, L0_ffn_ln_b, L0_ffn_W1, L0_ffn_b1, L0_ffn_W2, L0_ffn_b2):
    x = np.ascontiguousarray(L0_attn_x, dtype=np.float32)
    wvt = np.ascontiguousarray(np.asarray(L0_attn_Wv, dtype=np.float32).T)
    wot = np.ascontiguousarray(np.asarray(L0_attn_Wo, dtype=np.float32).T)
    w1 = np.ascontiguousarray(L0_ffn_W1, dtype=np.float32)
    w2 = np.ascontiguousarray(L0_ffn_W2, dtype=np.float32)
    bv_row = np.ascontiguousarray(np.asarray(L0_attn_bv, dtype=np.float32)[None, :])
    bo_row = np.ascontiguousarray(np.asarray(L0_attn_bo, dtype=np.float32)[None, :])
    b1_row = np.ascontiguousarray(np.asarray(L0_ffn_b1, dtype=np.float32)[None, :])
    b2_row = np.ascontiguousarray(np.asarray(L0_ffn_b2, dtype=np.float32)[None, :])
    lnw_row = np.ascontiguousarray(np.asarray(L0_ffn_ln_w, dtype=np.float32)[None, :])
    lnb_row = np.ascontiguousarray(np.asarray(L0_ffn_ln_b, dtype=np.float32)[None, :])
    identb = np.eye(P, dtype=ml_dtypes.bfloat16)
    onesf = np.ones((1, 512), dtype=np.float32)
    ones_col = np.ones((P, 1), dtype=np.float32)

    nc = build_nc()
    in_maps = []
    for c in range(8):
        b = c // 4
        r0 = (c % 4) * NTOK
        in_maps.append({
            "xg": np.ascontiguousarray(x[b]),
            "xo": np.ascontiguousarray(x[b, r0:r0 + NTOK]),
            "wvt": wvt, "wot": wot, "w1": w1, "w2": w2,
            "bv_row": bv_row, "bo_row": bo_row, "b1_row": b1_row, "b2_row": b2_row,
            "lnw_row": lnw_row, "lnb_row": lnb_row,
            "identb": identb, "onesf": onesf, "ones_col": ones_col,
        })
    res = run_bass_kernel_spmd(nc, in_maps, core_ids=list(range(8)))
    parts = [res.results[c]["out"] for c in range(8)]
    outb0 = np.concatenate(parts[0:4], axis=0)
    outb1 = np.concatenate(parts[4:8], axis=0)
    return np.stack([outb0, outb1], axis=0)


# version 24 (final)
# speedup vs baseline: 1.0135x; 1.0135x over previous
import numpy as np
import ml_dtypes
from contextlib import ExitStack

import concourse.bass as bass
import concourse.mybir as mybir
from concourse.bass_utils import run_bass_kernel_spmd

F32 = mybir.dt.float32
BF16 = mybir.dt.bfloat16
I16 = mybir.dt.int16
I32 = mybir.dt.int32
AF = mybir.ActivationFunctionType
ALU = mybir.AluOpType
AX = mybir.AxisListType

P = 128
D = 2048
H4 = 8192
NTOK = 1024
NT = NTOK // P
NB = 4096
NGT = NB // P
KT = D // P
OC = 512
NOC = H4 // OC
OT = OC // P
CT = D // P
EPS = 1e-5
MAGIC_MANT = 0x400000
THR_MANT = 0x7E0000
_PHASES = 4
_SUBA = 99
_DEBUG = False


class Sch:
    ENGS = ("T", "V", "S", "Q", "G", "H")

    def __init__(self, nc, stack):
        self.sem = {k: stack.enter_context(nc.semaphore("sem" + k)) for k in self.ENGS}
        self.cnt = {k: 0 for k in self.ENGS}
        self.waited = {(a, b): 0 for a in self.ENGS for b in self.ENGS}

    def mark(self, key, instr, n=1):
        instr.then_inc(self.sem[key], n)
        self.cnt[key] += n
        return self.cnt[key]

    def step(self, eng, key, instr, n=1):
        if key == "V":
            eng.tensor_copy(out=self.vdum[:, 2:3], in_=self.vdum[:, 0:1])
        elif key == "S":
            eng.copy(out=self.sdum[:, 2:3], in_=self.sdum[:, 0:1])
        else:
            c = self.mark(key, instr, n)
            eng.wait_ge(self.sem[key], c)
        return self.cnt[key]

    def wait(self, eng, skey, dkey, val):
        if not val or val <= 0:
            return
        if self.waited[(skey, dkey)] >= val:
            return
        eng.wait_ge(self.sem[dkey], val)
        self.waited[(skey, dkey)] = val


def build_nc():
    nc = bass.Bass("TRN2", target_bir_lowering=False, detect_race_conditions=False)

    xg = nc.dram_tensor("xg", [NB, D], F32, kind="ExternalInput").ap()
    xo = nc.dram_tensor("xo", [NTOK, D], F32, kind="ExternalInput").ap()
    wvt = nc.dram_tensor("wvt", [D, D], F32, kind="ExternalInput").ap()
    wot = nc.dram_tensor("wot", [D, D], F32, kind="ExternalInput").ap()
    w1 = nc.dram_tensor("w1", [H4, D], F32, kind="ExternalInput").ap()
    w2 = nc.dram_tensor("w2", [D, H4], F32, kind="ExternalInput").ap()
    bv_row = nc.dram_tensor("bv_row", [1, D], F32, kind="ExternalInput").ap()
    bo_row = nc.dram_tensor("bo_row", [1, D], F32, kind="ExternalInput").ap()
    b1_row = nc.dram_tensor("b1_row", [1, H4], F32, kind="ExternalInput").ap()
    b2_row = nc.dram_tensor("b2_row", [1, D], F32, kind="ExternalInput").ap()
    lnw_row = nc.dram_tensor("lnw_row", [1, D], F32, kind="ExternalInput").ap()
    lnb_row = nc.dram_tensor("lnb_row", [1, D], F32, kind="ExternalInput").ap()
    identb = nc.dram_tensor("identb", [P, P], BF16, kind="ExternalInput").ap()
    onesf = nc.dram_tensor("onesf", [1, 512], F32, kind="ExternalInput").ap()
    ones_col = nc.dram_tensor("ones_col", [P, 1], F32, kind="ExternalInput").ap()
    out = nc.dram_tensor("out", [NTOK, D], F32, kind="ExternalOutput").ap()

    if _DEBUG:
        dbg_attn = nc.dram_tensor("dbg_attn", [P, D], F32, kind="ExternalOutput").ap()
        dbg_lncol = nc.dram_tensor("dbg_lncol", [P, KT], F32, kind="ExternalOutput").ap()
        dbg_x1 = nc.dram_tensor("dbg_x1", [P, D], F32, kind="ExternalOutput").ap()
        dbg_xf = nc.dram_tensor("dbg_xf", [P, D], I16, kind="ExternalOutput").ap()
        dbg_qht = nc.dram_tensor("dbg_qht", [P, KT * P], I16, kind="ExternalOutput").ap()
        dbg_y2 = nc.dram_tensor("dbg_y2", [P, D], F32, kind="ExternalOutput").ap()
    x1d = nc.dram_tensor("x1d", [NTOK, D], F32, kind="Internal").ap()
    xfd = nc.dram_tensor("xfd", [NTOK, D], I16, kind="Internal").ap()
    qhtd = nc.dram_tensor("qhtd", [NTOK, KT * P], I16, kind="Internal").ap()

    stack = ExitStack()
    with stack:
        sch = Sch(nc, stack)
        T, V, S, Q, G, Hs = "T", "V", "S", "Q", "G", "H"

        def sb(name, shape, dt):
            return stack.enter_context(nc.sbuf_tensor(name, shape, dt))

        s_identb = sb("s_identb", [P, P], BF16)
        s_onesf = sb("s_onesf", [1, 512], F32)
        s_onescol = sb("s_onescol", [P, 1], F32)
        s_eps = sb("s_eps", [P, 1], F32)
        s_rowbuf = sb("s_rowbuf", [1, OC + D], F32)
        s_rowacc = sb("s_rowacc", [1, D], F32)

        s_stat = sb("s_stat", [P, 16], F32)
        s_corr = sb("s_corr", [1, 2], F32)
        s_lncol = sb("s_lncol", [P, KT], F32)
        s_vcol = sb("s_vcol", [P, KT], F32)
        s_lnrow = s_rowacc
        s_vrow = s_rowacc
        s_attnrow = s_rowacc

        s_x1a = sb("s_x1a", [P, D], F32)
        s_x1b = sb("s_x1b", [P, D], F32)
        s_xf = sb("s_xf", [P, D], I16)
        s_t2 = sb("s_t2", [P, D], F32)
        s_h = sb("s_h", [P, D], I16)
        s_qa = sb("s_qa", [P, D], F32)
        s_qb = sb("s_qb", [P, D], F32)
        s_qh = sb("s_qh", [P, D], I16)
        s_qhT = sb("s_qhT", [P, KT * P], I16)
        s_bstat = sb("s_bstat", [P, 16], F32)
        s_inj = sb("s_inj", [P, 8], I32)
        s_vdum = sb("s_vdum", [P, 4], F32)
        s_sdum = sb("s_sdum", [P, 4], F32)
        sch.vdum = s_vdum
        sch.sdum = s_sdum
        s_eb = sb("s_eb", [P, 64], I32)
        s_mb = sb("s_mb", [P, 64], I32)
        s_thr = sb("s_thr", [P, 64], I32)

        nb_h = D // 32
        nb_g = OC // 32

        s_w1a = sb("s_w1a", [P, D], F32)
        s_w1b = sb("s_w1b", [P, D], F32)
        s_qw = sb("s_qw", [P, D], I16)
        s_qw1t = sb("s_qw1t", [P, KT * OC], I16)
        s_qw2t = sb("s_qw2t", [P, OT * D], I16)
        s_y2 = sb("s_y2", [P, NT * D], F32)
        s_qht_l = sb("s_qht_l", [P, KT * P], I16)
        s_qht_l2 = sb("s_qht_l2", [P, KT * P], I16)
        s_h1 = s_qh[:, 0:OC]
        s_g = s_qh[:, OC:2 * OC]
        s_qg = s_qh[:, 2 * OC:3 * OC]
        s_qgT = s_qh[:, 3 * OC:4 * OC]
        s_b1rep = None

        s_b1rep = s_t2[:, 0:OC]
        s_lnwrep = s_y2[:, 2 * D:3 * D]
        s_lnbrep = s_y2[:, 3 * D:4 * D]
        s_attnrep = s_y2[:, 4 * D:5 * D]
        s_junk = s_qb
        s_s1 = s_qa
        s_w2a, s_w2b = s_x1a, s_x1b
        s_wv, s_wv2 = s_y2[:, 5 * D:6 * D], s_y2[:, 6 * D:7 * D]
        s_xga, s_xgb = s_y2[:, 0:D], s_y2[:, D:2 * D]

        with nc.Block() as block:

            def emit(engattr, fn):
                getattr(block, engattr)(fn)

            def barrier(keys=("T", "V", "S", "Q", "G")):
                snap = dict(sch.cnt)
                attr = {"T": "tensor", "V": "vector", "S": "scalar",
                        "Q": "sync", "G": "gpsimd"}
                for k in keys:
                    def _b(e, k=k, snap=snap):
                        for d in keys:
                            if d != k:
                                sch.wait(e, k, d, snap[d])
                    emit(attr[k], _b)

            def emit_quant(v, in_val_ap, n, nblk, out_i16_ap):
                sch.step(v, V, v.tensor_reduce(
                    out=s_qa[:, 0:nblk],
                    in_=in_val_ap.rearrange("p (b k) -> p b k", k=32),
                    axis=AX.X, op=ALU.max, apply_absolute_value=True))
                sch.step(v, V, v.tensor_scalar(
                    out=s_eb[:, 0:nblk], in0=s_qa[:, 0:nblk].bitcast(I32),
                    scalar1=23, scalar2=None, op0=ALU.logical_shift_right))
                sch.step(v, V, v.tensor_scalar(
                    out=s_eb[:, 0:nblk], in0=s_eb[:, 0:nblk],
                    scalar1=64, scalar2=17, op0=ALU.max, op1=ALU.add))
                sch.step(v, V, v.tensor_scalar(
                    out=s_mb[:, 0:nblk], in0=s_eb[:, 0:nblk],
                    scalar1=23, scalar2=MAGIC_MANT,
                    op0=ALU.logical_shift_left, op1=ALU.bitwise_or))
                sch.step(v, V, v.tensor_scalar(
                    out=s_thr[:, 0:nblk], in0=s_eb[:, 0:nblk],
                    scalar1=17, scalar2=None, op0=ALU.subtract))
                sch.step(v, V, v.tensor_scalar(
                    out=s_thr[:, 0:nblk], in0=s_thr[:, 0:nblk],
                    scalar1=23, scalar2=THR_MANT,
                    op0=ALU.logical_shift_left, op1=ALU.bitwise_or))
                thr_b = s_thr[:, 0:nblk].bitcast(F32).unsqueeze(-1).broadcast_to([P, nblk, 32])
                mb_b = s_mb[:, 0:nblk].bitcast(F32).unsqueeze(-1).broadcast_to([P, nblk, 32])
                in3 = in_val_ap.rearrange("p (b k) -> p b k", k=32)
                qa3 = s_qa[:, 0:n].rearrange("p (b k) -> p b k", k=32)
                qb3 = s_qb[:, 0:n].rearrange("p (b k) -> p b k", k=32)
                o3 = out_i16_ap.bitcast(BF16).rearrange("p (b k) -> p b k", k=32)
                sch.step(v, V, v.tensor_tensor(out=qa3, in0=in3, in1=thr_b, op=ALU.min))
                sch.step(v, V, v.tensor_tensor(out=qb3, in0=qa3, in1=mb_b, op=ALU.add))
                return v.tensor_tensor(out=o3, in0=qb3, in1=mb_b, op=ALU.subtract)

            te_bank_mark = {}

            def transpose_evac(ptrs, ntile, src_fn, dst_fn, src_ready_v=None,
                               dst_batch_fn=None):
                ngrp = (ntile + 3) // 4
                for grp in range(ngrp):
                    lo = grp * 4
                    hi = min(lo + 4, ntile)
                    ptr = ptrs[grp % len(ptrs)]

                    def _t(t, lo=lo, hi=hi, ptr=ptr, src_ready_v=src_ready_v):
                        if src_ready_v is not None:
                            sch.wait(t, T, V, src_ready_v)
                        sch.wait(t, T, S, te_bank_mark.get(id(ptr), 0))
                        mm = None
                        for i in range(lo, hi):
                            mm = t.transpose(ptr[:, (i - lo) * P:(i - lo) * P + P],
                                             src_fn(i), s_identb[:])
                        sch.mark(T, mm)
                    emit("tensor", _t)
                    te = sch.cnt[T]

                    def _s(sc, lo=lo, hi=hi, te=te, ptr=ptr):
                        sch.wait(sc, S, T, te)
                        if dst_batch_fn is not None:
                            mm = sc.copy(out=dst_batch_fn(lo, hi),
                                         in_=ptr[:, 0:(hi - lo) * P])
                        else:
                            mm = None
                            for i in range(lo, hi):
                                mm = sc.copy(out=dst_fn(i),
                                             in_=ptr[:, (i - lo) * P:(i - lo) * P + P])
                        sch.mark(S, mm)
                    emit("scalar", _s)
                    te_bank_mark[id(ptr)] = sch.cnt[S]

            def _q0(q):
                sch.mark(Q, q.dma_start(out=s_identb[:], in_=identb), 16)
                sch.mark(Q, q.dma_start(out=s_onesf[:], in_=onesf), 16)
                sch.mark(Q, q.dma_start(out=s_onescol[:], in_=ones_col), 16)
            emit("sync", _q0)

            def _g0(g):
                g.memset(s_vdum[:], 0.0)
                g.memset(s_sdum[:], 0.0)
                sch.mark(Hs, g.memset(s_eps[:], EPS))
            emit("gpsimd", _g0)

            p_tw = stack.enter_context(nc.psum_tensor("p_tw", [P, 1024], BF16))
            p_tg = stack.enter_context(nc.psum_tensor("p_tg", [P, 1024], BF16))
            w1bufs = [s_w1a, s_w1b, s_t2]
            w2bufs = [s_w2a, s_w2b]
            qwbufs = [s_qw, s_xf]
            w1_V_done = {}
            w2_V_done = {}
            qw_T_free = [0, 0]
            qw_par = [0]
            qw_ready = {}

            def emit_w_chunk(oc):
                for jo in range(OT):
                    gi = oc * OT + jo
                    buf = w1bufs[gi % 3]

                    def _q(q, gi=gi, buf=buf):
                        if gi >= 3:
                            sch.wait(q, Q, V, w1_V_done.get(gi - 3, 0))
                        sch.mark(Q, q.dma_start(out=buf[:],
                                                in_=w1[gi * P:(gi + 1) * P, :]), 16)
                    emit("sync", _q)
                    qd = sch.cnt[Q]

                    par = qw_par[0]
                    qw_par[0] ^= 1
                    qwb = qwbufs[par]

                    def _v(v, buf=buf, qd=qd, tf=qw_T_free[par], qwb=qwb):
                        sch.wait(v, V, Q, qd)
                        sch.wait(v, V, T, tf)
                        sch.mark(V, emit_quant(v, buf[:], D, nb_h, qwb[:]))
                    emit("vector", _v)
                    w1_V_done[gi] = sch.cnt[V]

                    transpose_evac(
                        [p_tw, p_tg], KT,
                        lambda i, qwb=qwb: qwb[:, i * P:(i + 1) * P].bitcast(BF16),
                        lambda i, jo=jo: s_qw1t[:, i * OC + jo * P:
                                                i * OC + jo * P + P].bitcast(BF16),
                        src_ready_v=sch.cnt[V],
                        dst_batch_fn=lambda lo, hi, jo=jo: s_qw1t[:].bitcast(BF16)
                            .rearrange("p (k c) -> p k c", c=OC)
                            [:, lo:hi, jo * P:jo * P + P])
                    qw_T_free[par] = sch.cnt[T]

                for jb in range(CT // 4):
                    buf = w2bufs[jb % 2]

                    def _q(q, jb=jb, buf=buf, oc=oc):
                        if jb >= 2:
                            sch.wait(q, Q, V, w2_V_done.get((oc, jb - 2), 0))
                        mm = None
                        for jl in range(4):
                            jc = jb * 4 + jl
                            mm = q.dma_start(
                                out=buf[:, jl * OC:(jl + 1) * OC],
                                in_=w2[jc * P:(jc + 1) * P, oc * OC:(oc + 1) * OC])
                            sch.mark(Q, mm, 16)
                    emit("sync", _q)
                    qd = sch.cnt[Q]

                    par = qw_par[0]
                    qw_par[0] ^= 1
                    qwb = qwbufs[par]

                    def _v(v, buf=buf, qd=qd, tf=qw_T_free[par], qwb=qwb):
                        sch.wait(v, V, Q, qd)
                        sch.wait(v, V, T, tf)
                        sch.mark(V, emit_quant(v, buf[:], D, 4 * nb_g, qwb[:]))
                    emit("vector", _v)
                    w2_V_done[(oc, jb)] = sch.cnt[V]

                    transpose_evac(
                        [p_tw, p_tg], 4 * OT,
                        lambda i, qwb=qwb: qwb[:, i * P:(i + 1) * P].bitcast(BF16),
                        lambda i, jb=jb: s_qw2t[:, (i % 4) * D + (jb * 4 + i // 4) * P:
                                                (i % 4) * D + (jb * 4 + i // 4) * P + P]
                        .bitcast(BF16),
                        src_ready_v=sch.cnt[V],
                        dst_batch_fn=lambda lo, hi, jb=jb: s_qw2t[:].bitcast(BF16)
                            .rearrange("p (k c) -> p k c", c=D)
                            [:, 0:hi - lo, (jb * 4 + lo // 4) * P:(jb * 4 + lo // 4) * P + P])
                    qw_T_free[par] = sch.cnt[T]

                qw_ready[oc] = sch.cnt[S]

            with (
                nc.psum_tensor("p_a", [P, D], F32) as p_a,
                nc.psum_tensor("p_b", [P, 512], F32) as p_b,
            ):
                for (dst, src_hbm) in (
                    (s_lnwrep, lnw_row),
                    (s_lnbrep, lnb_row),
                ):
                    def _q(q, src_hbm=src_hbm):
                        sch.wait(q, Q, T, sch.cnt[T])
                        sch.mark(Q, q.dma_start(out=s_rowbuf[0:1, 0:D], in_=src_hbm), 16)
                    emit("sync", _q)
                    qd = sch.cnt[Q]

                    def _t(t, qd=qd):
                        sch.wait(t, T, Q, qd)
                        sch.wait(t, T, G, sch.cnt[G])
                        sch.wait(t, T, V, sch.cnt[V])
                        mm = None
                        for j in range(4):
                            mm = t.matmul(p_a[:, j * 512:(j + 1) * 512], s_onesf[0:1, 0:P],
                                          s_rowbuf[0:1, j * 512:(j + 1) * 512],
                                          start=True, stop=True, skip_group_check=True)
                        sch.mark(T, mm)
                    emit("tensor", _t)

                    def _v(v, dst=dst):
                        sch.wait(v, V, T, sch.cnt[T])
                        mm = None
                        for j in range(4):
                            mm = v.tensor_copy(out=dst[:, j * 512:(j + 1) * 512],
                                               in_=p_a[:, j * 512:(j + 1) * 512])
                        sch.mark(V, mm)
                    emit("vector", _v)

                xgbufs = [s_xga, s_xgb]
                tile_T_done = [0] * NGT

                for jt in range(NGT if _SUBA >= 1 else 0):
                    buf = xgbufs[jt % 2]

                    def _q(q, jt=jt, buf=buf):
                        if jt >= 2:
                            sch.wait(q, Q, T, tile_T_done[jt - 2])
                        sch.mark(Q, q.dma_start(out=buf[:], in_=xg[jt * P:(jt + 1) * P, :]), 16)
                    emit("sync", _q)
                    qd = sch.cnt[Q]

                    def _v(v, jt=jt, buf=buf, qd=qd):
                        sch.wait(v, V, "H", 1)
                        sch.wait(v, V, Q, qd)
                        if jt >= 1:
                            sch.wait(v, V, T, tile_T_done[jt - 1])
                        sch.mark(V, v.tensor_reduce(out=s_stat[:, 0:1], in_=buf[:],
                                                    axis=AX.X, op=ALU.add))
                    emit("vector", _v)
                    vd1 = sch.cnt[V]

                    def _s(s, jt=jt, buf=buf, qd=qd, vd1=vd1):
                        sch.wait(s, S, Q, qd)
                        sch.wait(s, S, V, vd1)
                        sch.mark(S, s.activation(out=s_junk[:], in_=buf[:], func=AF.Square,
                                                 accum_out=s_stat[:, 1:2]))
                    emit("scalar", _s)
                    sd1 = sch.cnt[S]

                    def _v2(v, sd1=sd1):
                        sch.wait(v, V, S, sd1)
                        v.tensor_scalar(out=s_stat[:, 2:3], in0=s_stat[:, 0:1],
                                        scalar1=1.0 / D, scalar2=None, op0=ALU.mult)
                        sch.step(v, V, v.tensor_scalar(out=s_stat[:, 3:4], in0=s_stat[:, 1:2],
                                                       scalar1=1.0 / D, scalar2=None, op0=ALU.mult))
                        sch.step(v, V, v.tensor_tensor(out=s_stat[:, 5:6], in0=s_stat[:, 2:3],
                                                       in1=s_stat[:, 2:3], op=ALU.mult))
                        sch.mark(V, v.tensor_tensor(out=s_stat[:, 4:5], in0=s_stat[:, 3:4],
                                                    in1=s_stat[:, 5:6], op=ALU.subtract))
                    emit("vector", _v2)
                    vd2 = sch.cnt[V]

                    def _s2(s, vd2=vd2):
                        sch.wait(s, S, "H", 1)
                        sch.wait(s, S, V, vd2)
                        sch.mark(S, s.activation(out=s_stat[:, 5:6], in_=s_stat[:, 4:5],
                                                 func=AF.Sqrt, bias=s_eps[:], scale=1.0))
                    emit("scalar", _s2)
                    sd2 = sch.cnt[S]

                    def _v3(v, sd2=sd2):
                        sch.wait(v, V, S, sd2)
                        sch.step(v, V, v.reciprocal(out=s_stat[:, 6:7], in_=s_stat[:, 5:6]))
                        sch.step(v, V, v.tensor_scalar(out=s_stat[:, 7:8], in0=s_stat[:, 6:7],
                                                       scalar1=1.0 / NB, scalar2=None, op0=ALU.mult))
                        sch.mark(V, v.tensor_tensor(out=s_stat[:, 8:9], in0=s_stat[:, 7:8],
                                                    in1=s_stat[:, 2:3], op=ALU.mult))
                    emit("vector", _v3)
                    vd3 = sch.cnt[V]

                    def _t(t, jt=jt, buf=buf, vd3=vd3):
                        sch.wait(t, T, V, vd3)
                        st = (jt == 0)
                        sp = (jt == NGT - 1)
                        for j in range(4):
                            t.matmul(p_a[0:1, j * 512:(j + 1) * 512], s_stat[:, 7:8],
                                     buf[:, j * 512:(j + 1) * 512], start=st, stop=sp,
                                     skip_group_check=True)
                        sch.mark(T, t.matmul(p_b[0:1, 300:301], s_stat[:, 8:9], s_onescol[:],
                                             start=st, stop=sp, skip_group_check=True))
                    emit("tensor", _t)
                    tile_T_done[jt] = sch.cnt[T]

                def _v4(v):
                    if _SUBA < 2:
                        return
                    sch.wait(v, V, T, sch.cnt[T])
                    for j in range(4):
                        v.tensor_copy(out=s_lnrow[0:1, j * 512:(j + 1) * 512],
                                      in_=p_a[0:1, j * 512:(j + 1) * 512])
                    sch.step(v, V, v.tensor_copy(out=s_corr[0:1, 0:1], in_=p_b[0:1, 300:301]))
                    sch.mark(V, v.tensor_scalar(out=s_corr[0:1, 1:2], in0=s_corr[0:1, 0:1],
                                                scalar1=-1.0, scalar2=None, op0=ALU.mult))
                emit("vector", _v4)

                def _t4(t):
                    if _SUBA < 3:
                        return
                    sch.wait(t, T, V, sch.cnt[V])
                    mm = None
                    for jc in range(KT):
                        t.matmul(p_b[:, jc:jc + 1], s_lnrow[0:1, jc * P:(jc + 1) * P],
                                 s_onesf[0:1, 0:1], start=True, stop=False,
                                 skip_group_check=True)
                        mm = t.matmul(p_b[:, jc:jc + 1], s_onesf[0:1, 0:P],
                                      s_corr[0:1, 1:2], start=False, stop=True,
                                      skip_group_check=True)
                    sch.mark(T, mm)
                emit("tensor", _t4)

                def _v5(v):
                    if _SUBA < 3:
                        return
                    sch.wait(v, V, T, sch.cnt[T])
                    sch.mark(V, v.tensor_copy(out=s_lncol[:], in_=p_b[:, 0:KT]))
                emit("vector", _v5)

                if _PHASES >= 3:
                    emit_w_chunk(0)

                wvbufs = [s_wv, s_wv2]
                wv_T_done = [0] * KT
                for jc in range(KT if _SUBA >= 4 else 0):
                    buf = wvbufs[jc % 2]

                    def _q(q, jc=jc, buf=buf):
                        if jc >= 2:
                            sch.wait(q, Q, T, wv_T_done[jc - 2])
                        sch.mark(Q, q.dma_start(out=buf[:], in_=wvt[jc * P:(jc + 1) * P, :]), 16)
                    emit("sync", _q)
                    qd = sch.cnt[Q]

                    def _t(t, jc=jc, buf=buf, qd=qd):
                        sch.wait(t, T, Q, qd)
                        sch.wait(t, T, V, sch.cnt[V])
                        st = (jc == 0)
                        sp = (jc == KT - 1)
                        mm = None
                        for j in range(4):
                            mm = t.matmul(p_a[0:1, j * 512:(j + 1) * 512], s_lncol[:, jc:jc + 1],
                                          buf[:, j * 512:(j + 1) * 512], start=st, stop=sp,
                                          skip_group_check=True)
                        sch.mark(T, mm)
                    emit("tensor", _t)
                    wv_T_done[jc] = sch.cnt[T]

                def _qbv(q):
                    if _SUBA < 4:
                        return
                    sch.wait(q, Q, T, sch.cnt[T])
                    sch.mark(Q, q.dma_start(out=s_rowbuf[0:1, 0:D], in_=bv_row), 16)
                emit("sync", _qbv)
                qbv = sch.cnt[Q]

                def _v6(v, qbv=qbv):
                    if _SUBA < 4:
                        return
                    sch.wait(v, V, T, sch.cnt[T])
                    sch.wait(v, V, Q, qbv)
                    mm = None
                    for j in range(4):
                        mm = v.tensor_copy(out=s_vrow[0:1, j * 512:(j + 1) * 512],
                                           in_=p_a[0:1, j * 512:(j + 1) * 512])
                    sch.mark(V, mm)
                emit("vector", _v6)

                def _t5(t):
                    if _SUBA < 4:
                        return
                    sch.wait(t, T, V, sch.cnt[V])
                    mm = None
                    for jc in range(KT):
                        t.matmul(p_b[:, 32 + jc:33 + jc], s_vrow[0:1, jc * P:(jc + 1) * P],
                                 s_onesf[0:1, 0:1], start=True, stop=False,
                                 skip_group_check=True)
                        mm = t.matmul(p_b[:, 32 + jc:33 + jc], s_rowbuf[0:1, jc * P:(jc + 1) * P],
                                      s_onesf[0:1, 0:1], start=False, stop=True,
                                      skip_group_check=True)
                    sch.mark(T, mm)
                emit("tensor", _t5)

                def _v7(v):
                    if _SUBA < 4:
                        return
                    sch.wait(v, V, T, sch.cnt[T])
                    sch.mark(V, v.tensor_copy(out=s_vcol[:], in_=p_b[:, 32:32 + KT]))
                emit("vector", _v7)

                wo_T_done = [0] * KT
                for jq in range(KT if _SUBA >= 5 else 0):
                    buf = wvbufs[jq % 2]

                    def _q(q, jq=jq, buf=buf):
                        prev = wo_T_done[jq - 2] if jq >= 2 else wv_T_done[-1]
                        sch.wait(q, Q, T, prev)
                        sch.mark(Q, q.dma_start(out=buf[:], in_=wot[jq * P:(jq + 1) * P, :]), 16)
                    emit("sync", _q)
                    qd = sch.cnt[Q]

                    def _t(t, jq=jq, buf=buf, qd=qd):
                        sch.wait(t, T, Q, qd)
                        sch.wait(t, T, V, sch.cnt[V])
                        st = (jq == 0)
                        sp = (jq == KT - 1)
                        mm = None
                        for j in range(4):
                            mm = t.matmul(p_a[0:1, j * 512:(j + 1) * 512], s_vcol[:, jq:jq + 1],
                                          buf[:, j * 512:(j + 1) * 512], start=st, stop=sp,
                                          skip_group_check=True)
                        sch.mark(T, mm)
                    emit("tensor", _t)
                    wo_T_done[jq] = sch.cnt[T]

                def _qbo(q):
                    if _SUBA < 5:
                        return
                    sch.wait(q, Q, V, sch.cnt[V])
                    sch.mark(Q, q.dma_start(out=s_rowbuf[0:1, 0:D], in_=bo_row), 16)
                emit("sync", _qbo)
                qbo = sch.cnt[Q]

                def _v8(v, qbo=qbo):
                    if _SUBA < 5:
                        return
                    sch.wait(v, V, T, sch.cnt[T])
                    sch.wait(v, V, Q, qbo)
                    mm = None
                    for j in range(4):
                        mm = v.tensor_copy(out=s_attnrow[0:1, j * 512:(j + 1) * 512],
                                           in_=p_a[0:1, j * 512:(j + 1) * 512])
                    sch.mark(V, mm)
                emit("vector", _v8)

                def _t6(t):
                    if _SUBA < 6:
                        return
                    sch.wait(t, T, V, sch.cnt[V])
                    mm = None
                    for j in range(4):
                        t.matmul(p_a[:, j * 512:(j + 1) * 512], s_onesf[0:1, 0:P],
                                 s_attnrow[0:1, j * 512:(j + 1) * 512],
                                 start=True, stop=False, skip_group_check=True)
                        mm = t.matmul(p_a[:, j * 512:(j + 1) * 512], s_onesf[0:1, 0:P],
                                      s_rowbuf[0:1, j * 512:(j + 1) * 512],
                                      start=False, stop=True, skip_group_check=True)
                    sch.mark(T, mm)
                emit("tensor", _t6)

                def _v9(v):
                    if _SUBA < 6:
                        return
                    sch.wait(v, V, T, sch.cnt[T])
                    mm = None
                    for j in range(4):
                        mm = v.tensor_copy(out=s_attnrep[:, j * 512:(j + 1) * 512],
                                           in_=p_a[:, j * 512:(j + 1) * 512])
                    sch.mark(V, mm)
                emit("vector", _v9)

            if _DEBUG:
                def _qdbg(q):
                    sch.wait(q, Q, V, sch.cnt[V])
                    sch.mark(Q, q.dma_start(out=dbg_attn, in_=s_attnrep[:]), 16)
                    sch.mark(Q, q.dma_start(out=dbg_lncol, in_=s_lncol[:]), 16)
                emit("sync", _qdbg)

            barrier()

            with (
                nc.psum_tensor("p_tr", [P, 1024], BF16) as p_tr,
                nc.psum_tensor("p_tr2", [P, 1024], BF16) as p_tr2,
            ):
                x1bufs = [s_x1a, s_x1b]
                xfbufs = [s_xf, s_qw]
                st1_S = {}
                st1_G = {}
                st2_S = {}
                st2_G = {}
                st2_T = {}
                st2_V = {}

                def stage1(jt):
                    xb = x1bufs[jt % 2]
                    xf = xfbufs[jt % 2]
                    par = jt % 2

                    def _q(q, jt=jt, xb=xb):
                        sch.wait(q, Q, G, st1_G.get(jt - 2, 0))
                        sch.mark(Q, q.dma_start(out=xb[:], in_=xo[jt * P:(jt + 1) * P, :]), 16)
                    emit("sync", _q)
                    qd = sch.cnt[Q]

                    def _v(v, xb=xb, qd=qd, jt=jt):
                        sch.wait(v, V, Q, qd)
                        sch.wait(v, V, S, st1_S.get(jt - 1, 0))
                        sch.step(v, V, v.tensor_tensor(out=xb[:], in0=xb[:],
                                                       in1=s_attnrep[:], op=ALU.add))
                        sch.mark(V, v.tensor_reduce(out=s_bstat[:, 0:1], in_=xb[:],
                                                    axis=AX.X, op=ALU.add))
                    emit("vector", _v)
                    vd = sch.cnt[V]

                    def _g(g, jt=jt, xb=xb, vd=vd):
                        sch.wait(g, G, V, vd)
                        sch.mark(G, g.dma_start(out=x1d[jt * P:(jt + 1) * P, :], in_=xb[:]), 16)
                    emit("gpsimd", _g)

                    def _s(s, xb=xb, vd=vd):
                        sch.wait(s, S, V, vd)
                        sch.mark(S, s.activation(out=s_w1a[:], in_=xb[:], func=AF.Square,
                                                 accum_out=s_bstat[:, 1:2]))
                    emit("scalar", _s)
                    sd = sch.cnt[S]

                    def _v2(v, sd=sd):
                        sch.wait(v, V, S, sd)
                        v.tensor_scalar(out=s_bstat[:, 2:3], in0=s_bstat[:, 0:1],
                                        scalar1=1.0 / D, scalar2=None, op0=ALU.mult)
                        sch.step(v, V, v.tensor_scalar(out=s_bstat[:, 3:4], in0=s_bstat[:, 1:2],
                                                       scalar1=1.0 / D, scalar2=None, op0=ALU.mult))
                        sch.step(v, V, v.tensor_tensor(out=s_bstat[:, 12:13], in0=s_bstat[:, 2:3],
                                                       in1=s_bstat[:, 2:3], op=ALU.mult))
                        sch.mark(V, v.tensor_tensor(out=s_bstat[:, 4:5], in0=s_bstat[:, 3:4],
                                                    in1=s_bstat[:, 12:13], op=ALU.subtract))
                    emit("vector", _v2)
                    vd2 = sch.cnt[V]

                    def _s2(s, vd2=vd2):
                        sch.wait(s, S, V, vd2)
                        sch.mark(S, s.activation(out=s_bstat[:, 5:6], in_=s_bstat[:, 4:5],
                                                 func=AF.Sqrt, bias=s_eps[:], scale=1.0))
                    emit("scalar", _s2)
                    sd2 = sch.cnt[S]

                    def _v3(v, sd2=sd2):
                        sch.wait(v, V, S, sd2)
                        sch.step(v, V, v.reciprocal(out=s_bstat[:, 6:7], in_=s_bstat[:, 5:6]))
                        sch.step(v, V, v.tensor_tensor(out=s_bstat[:, 13:14], in0=s_bstat[:, 2:3],
                                                       in1=s_bstat[:, 6:7], op=ALU.mult))
                        sch.mark(V, v.tensor_scalar(out=s_bstat[:, 7:8], in0=s_bstat[:, 13:14],
                                                    scalar1=-1.0, scalar2=None, op0=ALU.mult))
                    emit("vector", _v3)
                    vd3 = sch.cnt[V]

                    def _s3(s, xb=xb, vd3=vd3, xf=xf, par=par, jt=jt):
                        sch.wait(s, S, V, max(vd3, st2_V.get(jt - 2, 0)))
                        sch.wait(s, S, G, st1_G.get(jt - 2, 0))
                        sch.mark(S, s.activation(out=xf[:].bitcast(BF16), in_=xb[:],
                                                 func=AF.Identity, bias=s_bstat[:, 7:8],
                                                 scale=s_bstat[:, 6:7],
                                                 accum_out=s_bstat[:, 8 + par:9 + par]))
                    emit("scalar", _s3)
                    st1_S[jt] = sch.cnt[S]

                    def _g2(g, jt=jt, xf=xf):
                        sch.wait(g, G, S, st1_S[jt])
                        sch.mark(G, g.dma_start(out=xfd[jt * P:(jt + 1) * P, :], in_=xf[:]), 16)
                    emit("gpsimd", _g2)
                    st1_G[jt] = sch.cnt[G]

                def stage2(jt):
                    xf = xfbufs[jt % 2]
                    par = jt % 2

                    def _v4(v, jt=jt, par=par):
                        sch.wait(v, V, S, st1_S[jt])
                        sch.mark(V, v.tensor_scalar(out=s_inj[:, 1:2].bitcast(F32),
                                                    in0=s_bstat[:, 8 + par:9 + par],
                                                    scalar1=-1.0 / D, scalar2=None,
                                                    op0=ALU.mult))
                    emit("vector", _v4)
                    vd4 = sch.cnt[V]

                    def _s4(s, vd4=vd4, xf=xf):
                        sch.wait(s, S, V, vd4)
                        sch.step(s, S, s.activation(out=s_s1[:], in_=xf[:].bitcast(BF16),
                                                    func=AF.Identity,
                                                    bias=s_inj[:, 1:2].bitcast(F32),
                                                    scale=1.0))
                        sch.mark(S, s.activation(out=s_w1b[:], in_=s_s1[:], func=AF.Square,
                                                 accum_out=s_bstat[:, 10:11]))
                    emit("scalar", _s4)
                    sd4 = sch.cnt[S]

                    def _v5(v, sd4=sd4):
                        sch.wait(v, V, S, sd4)
                        sch.mark(V, v.tensor_scalar(out=s_bstat[:, 11:12],
                                                    in0=s_bstat[:, 10:11],
                                                    scalar1=1.0 / D, scalar2=None,
                                                    op0=ALU.mult))
                    emit("vector", _v5)
                    vd5 = sch.cnt[V]

                    def _s5(s, vd5=vd5):
                        sch.wait(s, S, V, vd5)
                        sch.mark(S, s.activation(out=s_bstat[:, 14:15], in_=s_bstat[:, 11:12],
                                                 func=AF.Sqrt, bias=s_eps[:], scale=1.0))
                    emit("scalar", _s5)
                    sd5 = sch.cnt[S]

                    def _v6(v, sd5=sd5):
                        sch.wait(v, V, S, sd5)
                        sch.mark(V, v.reciprocal(out=s_inj[:, 0:1].bitcast(F32),
                                                 in_=s_bstat[:, 14:15]))
                    emit("vector", _v6)
                    vd6 = sch.cnt[V]

                    def _s6(s, vd6=vd6):
                        sch.wait(s, S, V, vd6)
                        sch.mark(S, s.activation(out=s_t2[:], in_=s_s1[:], func=AF.Identity,
                                                 scale=s_inj[:, 0:1].bitcast(F32)))
                    emit("scalar", _s6)
                    sd6 = sch.cnt[S]

                    def _v7(v, sd6=sd6, jt=jt):
                        sch.wait(v, V, S, sd6)
                        sch.wait(v, V, T, st2_T.get(jt - 1, 0))
                        sch.step(v, V, v.tensor_tensor(out=s_t2[:], in0=s_t2[:],
                                                       in1=s_lnwrep[:], op=ALU.mult))
                        sch.step(v, V, v.tensor_tensor(out=s_h[:].bitcast(BF16), in0=s_t2[:],
                                                       in1=s_lnbrep[:], op=ALU.add))
                        sch.mark(V, emit_quant(v, s_h[:].bitcast(BF16), D, nb_h, s_qh[:]))
                    emit("vector", _v7)

                    def _sw(sc, jt=jt):
                        sch.wait(sc, S, G, st2_G.get(jt - 1, 0))
                    emit("scalar", _sw)

                    transpose_evac(
                        [p_tr, p_tr2], KT,
                        lambda i: s_qh[:, i * P:(i + 1) * P].bitcast(BF16),
                        lambda i: s_qhT[:, i * P:(i + 1) * P].bitcast(BF16),
                        src_ready_v=sch.cnt[V],
                        dst_batch_fn=lambda lo, hi: s_qhT[:, lo * P:hi * P].bitcast(BF16))
                    st2_T[jt] = sch.cnt[T]
                    st2_S[jt] = sch.cnt[S]
                    st2_V[jt] = sch.cnt[V]
                    sdq = sch.cnt[S]

                    def _g3(g, jt=jt, sdq=sdq):
                        sch.wait(g, G, S, sdq)
                        sch.mark(G, g.dma_start(out=qhtd[jt * P:(jt + 1) * P, :],
                                                in_=s_qhT[:]), 16)
                    emit("gpsimd", _g3)
                    st2_G[jt] = sch.cnt[G]

                for step_i in range(NT + 1 if _PHASES >= 2 else 0):
                    if step_i < NT:
                        stage1(step_i)
                    if step_i >= 1:
                        stage2(step_i - 1)

            if _DEBUG:
                def _qdbg2(q):
                    sch.wait(q, Q, G, sch.cnt[G])
                    sch.mark(Q, q.dma_start(out=dbg_x1, in_=x1d[0:P, :]), 16)
                    sch.mark(Q, q.dma_start(out=dbg_xf, in_=xfd[0:P, :]), 16)
                    sch.mark(Q, q.dma_start(out=dbg_qht, in_=qhtd[0:P, :]), 16)
                emit("sync", _qdbg2)

            barrier()

            with (
                nc.psum_tensor("p_y1", [P, OC], F32) as p_y1,
                nc.psum_tensor("p_y1b", [P, OC], F32) as p_y1b,
                nc.psum_tensor("p_y2", [P, D], F32) as p_y2,
            ):
                qhlbufs = [s_qht_l, s_qht_l2]
                p_y1s = [p_y1, p_y1b]
                ffn1_T_done = {}
                h1_mark = {}
                y2_mark = {}

                for oc in range(NOC if _PHASES >= 3 else 0):
                    if oc > 0:
                        emit_w_chunk(oc)
                    qw_ready_S = qw_ready[oc]
                    b1rep_v = 0

                    def _qb1(q, oc=oc):
                        sch.wait(q, Q, T, sch.cnt[T])
                        sch.mark(Q, q.dma_start(out=s_rowbuf[0:1, 0:OC],
                                                in_=b1_row[0:1, oc * OC:(oc + 1) * OC]), 16)
                        if oc == 0:
                            sch.mark(Q, q.dma_start(out=s_rowbuf[0:1, OC:OC + D],
                                                    in_=b2_row), 16)
                    emit("sync", _qb1)
                    qb1 = sch.cnt[Q]

                    def emit_ffn1(jt, oc=oc):
                        qhl = qhlbufs[jt % 2]

                        def _g(g, oc=oc, jt=jt, qhl=qhl):
                            if jt >= 2:
                                sch.wait(g, G, T, ffn1_T_done.get((oc, jt - 2), 0))
                            elif oc >= 1:
                                sch.wait(g, G, T, ffn1_T_done.get((oc - 1, NT - 2 + jt), 0))
                            sch.mark(G, g.dma_start(out=qhl[:],
                                                    in_=qhtd[jt * P:(jt + 1) * P, :]), 16)
                        emit("gpsimd", _g)
                        gd = sch.cnt[G]
                        py1 = p_y1s[jt % 2]

                        def _t(t, qhl=qhl, gd=gd, oc=oc, jt=jt, py1=py1, qb1=qb1):
                            sch.wait(t, T, G, gd)
                            sch.wait(t, T, Q, qb1)
                            sch.wait(t, T, S, max(qw_ready_S,
                                     h1_mark.get((oc, jt - 2),
                                     h1_mark.get((oc - 1, NT - 2 + jt), 0))))
                            t.matmul(py1[:, 0:512], s_onesf[0:1, 0:P],
                                     s_rowbuf[0:1, 0:512],
                                     start=True, stop=False, skip_group_check=True)
                            mm = None
                            for jk in range(KT):
                                sp = (jk == KT - 1)
                                lhs = qhl[:, jk * P:(jk + 1) * P].bitcast(BF16)
                                mm = t.matmul(py1[:, 0:512], lhs,
                                              s_qw1t[:, jk * OC:
                                                     jk * OC + 512].bitcast(BF16),
                                              start=False, stop=sp, skip_group_check=True)
                            sch.mark(T, mm)
                        emit("tensor", _t)
                        ffn1_T_done[(oc, jt)] = sch.cnt[T]

                    def emit_epi(jt, oc=oc):
                        td = ffn1_T_done[(oc, jt)]
                        py1 = p_y1s[jt % 2]

                        def _s(s, td=td, py1=py1):
                            sch.wait(s, S, T, td)
                            sch.step(s, S, s.copy(out=s_h1[:].bitcast(BF16),
                                                  in_=py1[:, 0:OC]))
                            sch.mark(S, s.activation(out=s_g[:].bitcast(BF16),
                                                     in_=s_h1[:].bitcast(BF16),
                                                     func=AF.Gelu_apprx_tanh))
                        emit("scalar", _s)
                        h1_mark[(oc, jt)] = sch.cnt[S]
                        sd = sch.cnt[S]

                        def _v2(v, sd=sd):
                            sch.wait(v, V, S, sd)
                            sch.mark(V, emit_quant(v, s_g[:].bitcast(BF16), OC, nb_g, s_qg[:]))
                        emit("vector", _v2)

                        transpose_evac(
                            [p_tw, p_tg], OT,
                            lambda i: s_qg[:, i * P:(i + 1) * P].bitcast(BF16),
                            lambda i: s_qgT[:, i * P:(i + 1) * P].bitcast(BF16),
                            src_ready_v=sch.cnt[V],
                            dst_batch_fn=lambda lo, hi: s_qgT[:, lo * P:hi * P].bitcast(BF16))
                        sd3 = sch.cnt[S]

                        t3_marks = []

                        def _t3(t, sd3=sd3, oc=oc, jt=jt):
                            sch.wait(t, T, S, sd3)
                            prev = y2_mark.get((oc, jt - 1),
                                   y2_mark.get((oc - 1, NT - 1), 0))
                            sch.wait(t, T, V, prev)
                            for jn in range(4):
                                if oc == 0:
                                    t.matmul(p_y2[:, jn * 512:(jn + 1) * 512],
                                             s_onesf[0:1, 0:P],
                                             s_rowbuf[0:1, OC + jn * 512:
                                                      OC + (jn + 1) * 512],
                                             start=True, stop=False,
                                             skip_group_check=True)
                                mm = None
                                for jo in range(OT):
                                    st = (jo == 0) and (oc != 0)
                                    sp = (jo == OT - 1)
                                    lhs = s_qgT[:, jo * P:(jo + 1) * P].bitcast(BF16)
                                    mm = t.matmul(p_y2[:, jn * 512:(jn + 1) * 512], lhs,
                                                  s_qw2t[:, jo * D + jn * 512:
                                                         jo * D + (jn + 1) * 512].bitcast(BF16),
                                                  start=st, stop=sp, skip_group_check=True)
                                t3_marks.append(sch.mark(T, mm))
                        emit("tensor", _t3)

                        def _v4(v, oc=oc, jt=jt, t3_marks=t3_marks):
                            dst = s_y2[:, jt * D:(jt + 1) * D]
                            mm = None
                            for j in range(4):
                                sch.wait(v, V, T, t3_marks[j])
                                sl = slice(j * 512, (j + 1) * 512)
                                if oc == 0:
                                    mm = v.tensor_copy(out=dst[:, sl], in_=p_y2[:, sl])
                                else:
                                    mm = v.tensor_tensor(out=dst[:, sl], in0=dst[:, sl],
                                                         in1=p_y2[:, sl], op=ALU.add)
                            sch.mark(V, mm)
                        emit("vector", _v4)
                        y2_mark[(oc, jt)] = sch.cnt[V]

                    for step in range(NT + 1):
                        if step < NT:
                            emit_ffn1(step)
                        if step >= 1:
                            emit_epi(step - 1)

            if _DEBUG:
                def _qdbg3(q):
                    sch.wait(q, Q, V, sch.cnt[V])
                    sch.mark(Q, q.dma_start(out=dbg_y2, in_=s_y2[:, 0:D]), 16)
                emit("sync", _qdbg3)

            d_V_done = [0] * NT
            for jt in range(NT if _PHASES >= 4 else 0):
                xb = [s_x1a, s_x1b][jt % 2]
                wb = [s_w1a, s_w1b][jt % 2]
                hb_bf = [s_h[:].bitcast(BF16),
                         s_w1a[:, D // 2:D].bitcast(BF16)][jt % 2]

                def _g(g, jt=jt, xb=xb, wb=wb):
                    gate = d_V_done[jt - 2] if jt >= 2 else y2_mark[(NOC - 1, jt)]
                    sch.wait(g, G, V, gate)
                    sch.mark(G, g.dma_start(out=xb[:], in_=x1d[jt * P:(jt + 1) * P, :]), 16)
                    sch.mark(G, g.dma_start(out=wb[:, 0:D // 2].bitcast(I16),
                                            in_=xfd[jt * P:(jt + 1) * P, :]), 16)
                emit("gpsimd", _g)
                gd = sch.cnt[G]

                def _sD(s, jt=jt, hb_bf=hb_bf):
                    gate = d_V_done[jt - 2] if jt >= 2 else y2_mark[(NOC - 1, jt)]
                    sch.wait(s, S, V, gate)
                    sch.mark(S, s.activation(out=hb_bf,
                                             in_=s_y2[:, jt * D:(jt + 1) * D],
                                             func=AF.Identity))
                emit("scalar", _sD)
                sdD = sch.cnt[S]

                def _v(v, jt=jt, xb=xb, wb=wb, hb_bf=hb_bf, gd=gd, sdD=sdD):
                    sch.wait(v, V, G, gd)
                    sch.wait(v, V, S, sdD)
                    sch.step(v, V, v.tensor_tensor(out=s_qh[:].bitcast(BF16),
                                                   in0=wb[:, 0:D // 2].bitcast(BF16),
                                                   in1=hb_bf, op=ALU.add))
                    sch.mark(V, v.tensor_tensor(out=s_t2[:], in0=xb[:],
                                                in1=s_qh[:].bitcast(BF16), op=ALU.add))
                emit("vector", _v)
                d_V_done[jt] = sch.cnt[V]
                vd = sch.cnt[V]

                def _q(q, jt=jt, vd=vd):
                    sch.wait(q, Q, V, vd)
                    sch.mark(Q, q.dma_start(out=out[jt * P:(jt + 1) * P, :], in_=s_t2[:]), 16)
                emit("sync", _q)
                qd2 = sch.cnt[Q]

                def _vw(v, qd2=qd2):
                    sch.wait(v, V, Q, qd2)
                emit("vector", _vw)

            if _PHASES < 4:
                for jt in range(NT):
                    def _q(q, jt=jt):
                        sch.wait(q, Q, V, sch.cnt[V])
                        sch.mark(Q, q.dma_start(out=out[jt * P:(jt + 1) * P, :],
                                                in_=s_t2[:]), 16)
                    emit("sync", _q)

            final_q = sch.cnt[Q]

            def _qf(q):
                q.wait_ge(sch.sem[Q], final_q)
            emit("sync", _qf)

    return nc


def kernel(L0_attn_x, L0_attn_Wq, L0_attn_bq, L0_attn_Wk, L0_attn_bk,
           L0_attn_Wv, L0_attn_bv, L0_attn_Wo, L0_attn_bo,
           L0_ffn_ln_w, L0_ffn_ln_b, L0_ffn_W1, L0_ffn_b1, L0_ffn_W2, L0_ffn_b2):
    x = np.ascontiguousarray(L0_attn_x, dtype=np.float32)
    wvt = np.ascontiguousarray(np.asarray(L0_attn_Wv, dtype=np.float32).T)
    wot = np.ascontiguousarray(np.asarray(L0_attn_Wo, dtype=np.float32).T)
    w1 = np.ascontiguousarray(L0_ffn_W1, dtype=np.float32)
    w2 = np.ascontiguousarray(L0_ffn_W2, dtype=np.float32)
    bv_row = np.ascontiguousarray(np.asarray(L0_attn_bv, dtype=np.float32)[None, :])
    bo_row = np.ascontiguousarray(np.asarray(L0_attn_bo, dtype=np.float32)[None, :])
    b1_row = np.ascontiguousarray(np.asarray(L0_ffn_b1, dtype=np.float32)[None, :])
    b2_row = np.ascontiguousarray(np.asarray(L0_ffn_b2, dtype=np.float32)[None, :])
    lnw_row = np.ascontiguousarray(np.asarray(L0_ffn_ln_w, dtype=np.float32)[None, :])
    lnb_row = np.ascontiguousarray(np.asarray(L0_ffn_ln_b, dtype=np.float32)[None, :])
    identb = np.eye(P, dtype=ml_dtypes.bfloat16)
    onesf = np.ones((1, 512), dtype=np.float32)
    ones_col = np.ones((P, 1), dtype=np.float32)

    nc = build_nc()
    in_maps = []
    for c in range(8):
        b = c // 4
        r0 = (c % 4) * NTOK
        in_maps.append({
            "xg": np.ascontiguousarray(x[b]),
            "xo": np.ascontiguousarray(x[b, r0:r0 + NTOK]),
            "wvt": wvt, "wot": wot, "w1": w1, "w2": w2,
            "bv_row": bv_row, "bo_row": bo_row, "b1_row": b1_row, "b2_row": b2_row,
            "lnw_row": lnw_row, "lnb_row": lnb_row,
            "identb": identb, "onesf": onesf, "ones_col": ones_col,
        })
    res = run_bass_kernel_spmd(nc, in_maps, core_ids=list(range(8)))
    parts = [res.results[c]["out"] for c in range(8)]
    outb0 = np.concatenate(parts[0:4], axis=0)
    outb1 = np.concatenate(parts[4:8], axis=0)
    return np.stack([outb0, outb1], axis=0)
